# version 09
# speedup vs baseline: 1.2446x; 1.0549x over previous
import numpy as np
import ml_dtypes
from contextlib import ExitStack

import concourse.bass as bass
import concourse.mybir as mybir
from concourse.bass_utils import run_bass_kernel_spmd

F32 = mybir.dt.float32
BF16 = mybir.dt.bfloat16
AF = mybir.ActivationFunctionType
ALU = mybir.AluOpType
AX = mybir.AxisListType

NSEQ = 2
_SKIP_LDW_BY_MODULE = {}
_CUR_SKIP_LDW = set()

_orig_to_json_bytes = bass.Bass.to_json_bytes

def _patched_to_json_bytes(self):
    b = _orig_to_json_bytes(self)
    names = _SKIP_LDW_BY_MODULE.get(id(self), None)
    if not names:
        return b
    import orjson
    j = orjson.loads(b)
    n = 0
    for f in j["functions"]:
        for blk in f["blocks"]:
            for ins in blk["instructions"]:
                if ins.get("opcode") == "Matmult" and ins.get("name") in names:
                    ins["ldweights"] = False
                    n += 1
    return orjson.dumps(j)

bass.Bass.to_json_bytes = _patched_to_json_bytes

T = 2048
D = 1024
NT = T // 128
ND = D // 128
NCORE = 8


class Sem:
    __slots__ = ("h", "v", "dma")
    def __init__(self, h):
        self.h = h
        self.v = 0
        self.dma = False


class Ctx:
    ENGS = ("t", "v", "s", "g", "y")

    def __init__(self, nc):
        self.nc = nc
        self.q = {k: [] for k in self.ENGS}
        self._sems = {}
        self.bar = self.sem("bar")
        self.ready = self.sem("ready")

    def sem(self, name):
        if name not in self._sems:
            self._sems[name] = Sem(self.nc.alloc_semaphore(name))
        return self._sems[name]

    def t(self, fn): self.q["t"].append(fn)
    def v(self, fn): self.q["v"].append(fn)
    def s(self, fn): self.q["s"].append(fn)
    def g(self, fn): self.q["g"].append(fn)
    def y(self, fn): self.q["y"].append(fn)

    def wait(self, eng, sem, val=None):
        tv = sem.v if val is None else val
        if tv <= 0:
            return
        self.q[eng].append(lambda e, h=sem.h, tv=tv: e.wait_ge(h, tv))

    def barrier(self):
        dma_waits = [(s.h, s.v) for s in self._sems.values() if s.dma and s.v > 0]
        def fy(e, dma_waits=dma_waits):
            for h, v in dma_waits:
                e.wait_ge(h, v)
        self.q["y"].append(fy)
        b = self.bar
        b.v += len(self.ENGS)
        tgt = b.v
        for k in self.ENGS:
            def f(e, tgt=tgt, h=b.h):
                e.drain()
                e.nop().then_inc(h, 1)
                e.wait_ge(h, tgt)
            self.q[k].append(f)

    def build(self, blk):
        qs = self.q
        @blk.tensor
        def _(e):
            for fn in qs["t"]: fn(e)
        @blk.vector
        def _(e):
            for fn in qs["v"]: fn(e)
        @blk.scalar
        def _(e):
            for fn in qs["s"]: fn(e)
        @blk.gpsimd
        def _(e):
            for fn in qs["g"]: fn(e)
        @blk.sync
        def _(e):
            for fn in qs["y"]: fn(e)


def _inc(sem, n=1):
    sem.v += n
    if n >= 16:
        sem.dma = True
    return sem.h



def _fm_bias(b):
    n = b.shape[0] // 128
    return np.ascontiguousarray(b.reshape(n, 128).T)


def prep_host(inputs):
    w = {}
    f32 = lambda a: np.ascontiguousarray(np.asarray(a), dtype=np.float32)
    w["w_qkvT"] = f32(inputs["L0_attn_w_qkv"].T)
    bq = f32(inputs["L0_attn_b_qkv"])
    w["bq_q_fm"] = _fm_bias(bq[:1024])
    w["bq_k_fm"] = _fm_bias(bq[1024:2048])
    w["bq_v_row"] = bq[2048:].reshape(1, 1024)
    w["w_projT"] = f32(inputs["L0_attn_w_proj"].T)
    w["b_proj_fm"] = _fm_bias(f32(inputs["L0_attn_b_proj"]))
    w["W1T"] = f32(inputs["L0_ffn_W1"].T)
    w["b1_fm"] = _fm_bias(f32(inputs["L0_ffn_b1"]))
    w["W2T"] = f32(inputs["L0_ffn_W2"].T)
    w["b2_fm"] = _fm_bias(f32(inputs["L0_ffn_b2"]))
    w["ln_scale_fm"] = _fm_bias(f32(inputs["L1_attn_ln_scale"]))
    w["ln_bias_fm"] = _fm_bias(f32(inputs["L1_attn_ln_bias"]))
    w["W_hidden"] = f32(inputs["L1_attn_W_hidden"])
    bh = f32(inputs["L1_attn_b_hidden"])
    w["bh_v_row"] = bh[:2048].reshape(1, 2048)
    w["bh_g_fm"] = _fm_bias(bh[2048:])
    w["W_qk"] = f32(inputs["L1_attn_W_qk"])
    w["b_qk_col"] = f32(inputs["L1_attn_b_qk"]).reshape(128, 1)
    osw = f32(inputs["L1_attn_os_weight"]); osb = f32(inputs["L1_attn_os_bias"])
    w["os_w0"] = osw[0].reshape(128, 1); w["os_b0"] = osb[0].reshape(128, 1)
    w["os_w1"] = osw[1].reshape(128, 1); w["os_b1"] = osb[1].reshape(128, 1)
    w["W_out"] = f32(inputs["L1_attn_W_out"])
    w["b_out_fm"] = _fm_bias(f32(inputs["L1_attn_b_out"]))
    w["gw1T"] = f32(inputs["L1_ffn_gw1"].T)
    w["gb1_fm"] = _fm_bias(f32(inputs["L1_ffn_gb1"]))
    w["gw2T"] = f32(inputs["L1_ffn_gw2"].T)
    w["gb2_row"] = f32(inputs["L1_ffn_gb2"]).reshape(1, 8)
    for p in ("q", "k", "v", "o"):
        w[p + "WmT"] = f32(inputs[f"L1_ffn_{p}Wm"].T)
        w[p + "Vt"] = f32(np.asarray(inputs[f"L1_ffn_{p}V"]).transpose(2, 0, 1).reshape(1024, 128))
        w[p + "U_ero"] = f32(np.asarray(inputs[f"L1_ffn_{p}U"]).transpose(0, 2, 1).reshape(128, 1024))
        w[p + "S_col"] = f32(inputs[f"L1_ffn_{p}S"]).reshape(128, 1)
        if p == "v":
            w[p + "b_row"] = f32(inputs[f"L1_ffn_{p}b"]).reshape(1, 1024)
        else:
            w[p + "b_fm"] = _fm_bias(f32(inputs[f"L1_ffn_{p}b"]))
    w["ident_f32"] = np.eye(128, dtype=np.float32)
    w["ident_bf16"] = np.eye(128, dtype=np.float32).astype(ml_dtypes.bfloat16)
    e16 = np.zeros((8, 128), dtype=np.float32)
    for e in range(8):
        e16[e, e * 16:(e + 1) * 16] = 1.0
    w["e16"] = e16
    return w


INTER = {
    "x_fm":  ((NSEQ, ND, 128, T), F32),
    "xn0":   ((NSEQ, ND, 128, T), BF16),
    "s0":    ((NSEQ, ND, 128, T), BF16),
    "x1_fm": ((NSEQ, ND, 128, T), F32),
    "xn1":   ((NSEQ, ND, 128, T), BF16),
    "pooled": ((NSEQ, ND, 128, 1), F32),
    "s1":    ((NSEQ, ND, 128, T), BF16),
    "Wq1T":  ((1024, 4096), BF16),
    "Wq2T":  ((4096, 1024), BF16),
    "out":   ((NSEQ, T, D), F32),
}

CORE_INPUTS = {
    "x": ((NSEQ, T, D), F32),
}



def emit_ln_from_fm(ctx, pref, xfm, xb, xn_out_sb, aux_sems, consts, pooled_sb=None):
    nc = ctx.nc
    (ones_col_b, ones_row_f, cpool, srow, qrow, sgrid, qgrid, m2g, varg, sqg,
     rgrid, murow, rrow, tmp, x2, ps_s, ps_q, ps_mu, ps_r) = consts

    sq_d = aux_sems["sq"]
    stat_mm = aux_sems["smm"]
    aux = aux_sems["aux"]
    dreg = aux_sems["dreg"]
    bc_d = aux_sems["bc"]
    nrm_d = aux_sems["nrm"]
    pool_d = aux_sems.get("pool")

    qmm_hist = []
    for tb in range(4):
        sl = slice(tb * 512, (tb + 1) * 512)
        if tb >= 1:
            ctx.wait("t", aux)
        for dc in range(ND):
            p = dc % 2
            if len(qmm_hist) >= 2:
                ctx.wait("s", stat_mm, qmm_hist[-2])
            h = _inc(sq_d, 1)
            ctx.s(lambda e, dc=dc, sl=sl, p=p, h=h:
                  e.activation(x2[p][:, :], xb[dc][:, sl], AF.Square).then_inc(h, 1))
            start = (dc == 0)
            stop = (dc == ND - 1)
            h1 = _inc(stat_mm, 1)
            ctx.t(lambda e, dc=dc, sl=sl, start=start, stop=stop, h=h1:
                  e.matmul(ps_s[:, :], ones_col_b[:, :], xb[dc][:, sl],
                           start=start, stop=stop).then_inc(h, 1))
            ctx.wait("t", sq_d)
            h2 = _inc(stat_mm, 1)
            ctx.t(lambda e, p=p, start=start, stop=stop, h=h2:
                  e.matmul(ps_q[:, :], ones_col_b[:, :], x2[p][:, :],
                           start=start, stop=stop).then_inc(h, 1))
            qmm_hist.append(stat_mm.v)
        ctx.wait("v", stat_mm)
        h = _inc(aux, 1)
        def cpr(e, sl=sl, h=h):
            e.tensor_copy(srow[:, sl], ps_s[:, :])
            e.tensor_copy(qrow[:, sl], ps_q[:, :]).then_inc(h, 1)
        ctx.v(cpr)
    ctx.wait("y", aux)
    h = _inc(dreg, 16)
    ctx.y(lambda e, h=h: e.dma_start(sgrid[:, :], srow[:, :]).then_inc(h, 16))
    h = _inc(dreg, 16)
    ctx.y(lambda e, h=h: e.dma_start(qgrid[:, :], qrow[:, :]).then_inc(h, 16))
    ctx.wait("v", dreg)
    h = _inc(aux, 1)
    ctx.v(lambda e, h=h: e.tensor_scalar(sgrid[:, :], sgrid[:, :], cpool[:, 0:1], None,
                                         ALU.mult).then_inc(h, 1))
    ctx.wait("v", aux)
    h = _inc(aux, 1)
    ctx.v(lambda e, h=h: e.tensor_mul(m2g[:, :], sgrid[:, :], sgrid[:, :]).then_inc(h, 1))
    ctx.wait("v", aux)
    h = _inc(aux, 1)
    ctx.v(lambda e, h=h: e.scalar_tensor_tensor(varg[:, :], qgrid[:, :], cpool[:, 0:1],
                                                m2g[:, :], ALU.mult, ALU.subtract).then_inc(h, 1))
    ctx.wait("v", aux)
    h = _inc(aux, 1)
    ctx.v(lambda e, h=h: e.tensor_scalar(varg[:, :], varg[:, :], cpool[:, 1:2], None,
                                         ALU.add).then_inc(h, 1))
    ctx.wait("s", aux)
    h = _inc(aux, 1)
    ctx.s(lambda e, h=h: e.activation(sqg[:, :], varg[:, :], AF.Sqrt).then_inc(h, 1))
    ctx.wait("v", aux)
    h = _inc(aux, 1)
    ctx.v(lambda e, h=h: e.reciprocal(rgrid[:, :], sqg[:, :]).then_inc(h, 1))
    ctx.wait("y", aux)
    h = _inc(dreg, 16)
    ctx.y(lambda e, h=h: e.dma_start(murow[:, :], sgrid[:, :]).then_inc(h, 16))
    h = _inc(dreg, 16)
    ctx.y(lambda e, h=h: e.dma_start(rrow[:, :], rgrid[:, :]).then_inc(h, 16))
    ctx.wait("t", dreg)
    for tb in range(4):
        sl = slice(tb * 512, (tb + 1) * 512)
        if tb >= 1:
            ctx.wait("t", nrm_d)
        h = _inc(bc_d, 1)
        def bc(e, sl=sl, h=h):
            e.matmul(ps_mu[:, :], ones_row_f[:, :], murow[:, sl], start=True, stop=True)
            e.matmul(ps_r[:, :], ones_row_f[:, :], rrow[:, sl], start=True, stop=True).then_inc(h, 1)
        ctx.t(bc)
        ctx.wait("v", bc_d)
        for dc in range(ND):
            cs = aux_sems["nrm"]
            cw0 = cs.v
            _inc(cs, 1)
            cw = cs.v
            _inc(cs, 1)
            h = cs.h
            def nrm(e, dc=dc, sl=sl, h=h, cw0=cw0, cw=cw):
                if cw0 > 0:
                    e.wait_ge(h, cw0)
                e.tensor_sub(tmp[:, :], xfm[dc][:, sl], ps_mu[:, :]).then_inc(h, 1)
                e.wait_ge(h, cw)
                e.tensor_mul(xn_out_sb[dc][:, sl], tmp[:, :], ps_r[:, :]).then_inc(h, 1)
            ctx.v(nrm)


def stage_prep(ctx, io):
    nc = ctx.nc
    x = io["x"]; x_fm_d = io["x_fm"]; xn0_d = io["xn0"]; ident = io["ident_f32"]
    st = ExitStack()
    E = st.enter_context
    xtok = [E(nc.sbuf_tensor(f"p_xtok{i}", [128, 1024], F32)) for i in range(2)]
    xfm = [E(nc.sbuf_tensor(f"p_xfm{c}", [128, T], F32)) for c in range(ND)]
    xb = [E(nc.sbuf_tensor(f"p_xb{c}", [128, T], BF16)) for c in range(ND)]
    x2 = [E(nc.sbuf_tensor(f"p_x2{i}", [128, 512], BF16)) for i in range(2)]
    idf = E(nc.sbuf_tensor("p_idf", [128, 128], F32))
    ones_col_b = E(nc.sbuf_tensor("p_ones_col_b", [128, 1], BF16))
    ones_row_f = E(nc.sbuf_tensor("p_ones_row_f", [1, 128], F32))
    srow = E(nc.sbuf_tensor("p_srow", [1, T], F32))
    qrow = E(nc.sbuf_tensor("p_qrow", [1, T], F32))
    sgrid = E(nc.sbuf_tensor("p_sgrid", [128, 16], F32))
    qgrid = E(nc.sbuf_tensor("p_qgrid", [128, 16], F32))
    m2g = E(nc.sbuf_tensor("p_m2g", [128, 16], F32))
    varg = E(nc.sbuf_tensor("p_varg", [128, 16], F32))
    sqg = E(nc.sbuf_tensor("p_sqg", [128, 16], F32))
    rgrid = E(nc.sbuf_tensor("p_rg", [128, 16], F32))
    murow = E(nc.sbuf_tensor("p_murow", [1, T], F32))
    rrow = E(nc.sbuf_tensor("p_rrow", [1, T], F32))
    tmp = E(nc.sbuf_tensor("p_tmp", [128, 512], F32))
    cpool = E(nc.sbuf_tensor("p_cpool", [128, 4], F32))
    pst = [E(nc.psum_tensor(f"p_pst{i}", [128, 512], F32)) for i in range(2)]
    ps_s = E(nc.psum_tensor("p_ps_s", [1, 512], F32))
    ps_q = E(nc.psum_tensor("p_ps_q", [1, 512], F32))
    ps_mu = E(nc.psum_tensor("p_ps_mu", [128, 512], F32))
    ps_r = E(nc.psum_tensor("p_ps_r", [128, 512], F32))

    dxin = [ctx.sem("p_dxin0"), ctx.sem("p_dxin1")]
    trd = ctx.sem("p_trd")
    evd = ctx.sem("p_evd")
    cast_d = ctx.sem("p_cast")
    dout = ctx.sem("p_dout")
    aux_sems = {k: ctx.sem("p_" + k) for k in ("sq", "smm", "aux", "dreg", "bc", "nrm")}

    ctx.g(lambda e: e.memset(ones_col_b[:, :], 1.0))
    ctx.g(lambda e: e.memset(ones_row_f[:, :], 1.0))
    ctx.g(lambda e: e.memset(cpool[:, 0:1], 1.0 / 1024.0))
    ctx.g(lambda e: e.memset(cpool[:, 1:2], 1e-5))
    dgld = ctx.sem("p_dgld")
    h = _inc(dgld, 16)
    ctx.g(lambda e, h=h: e.dma_start(idf[:, :], ident).then_inc(h, 16))
    ctx.wait("g", dgld)
    ctx.barrier()

    consts = (ones_col_b, ones_row_f, cpool, srow, qrow, sgrid, qgrid, m2g, varg,
              sqg, rgrid, murow, rrow, tmp, x2, ps_s, ps_q, ps_mu, ps_r)

    def _seq_body0(s):
        for tt in range(NT):
            p = tt % 2
            if tt >= 2:
                ctx.wait("y", trd, 2 * (tt - 1) + 32 * s)
            h = _inc(dxin[p], 16)
            ctx.y(lambda e, s=s, tt=tt, p=p, h=h:
                  e.dma_start(xtok[p][:, :], x[s, tt * 128:(tt + 1) * 128, :]).then_inc(h, 16))
            ctx.wait("t", dxin[p])
            for half in range(2):
                ev_need = (2 * tt + half - 1) + 32 * s
                if ev_need >= 1:
                    ctx.wait("t", evd, ev_need)
                h = _inc(trd, 1)
                def tr(e, p=p, half=half, h=h):
                    for j in range(4):
                        dc = half * 4 + j
                        ins = e.transpose(pst[half][:, j * 128:(j + 1) * 128],
                                          xtok[p][:, dc * 128:(dc + 1) * 128], idf[:, :])
                    ins.then_inc(h, 1)
                ctx.t(tr)
            for half in range(2):
                ctx.wait("v", trd, (2 * tt + half + 1) + 32 * s)
                h = _inc(evd, 1)
                def ev(e, tt=tt, half=half, h=h):
                    for j in range(4):
                        dc = half * 4 + j
                        ins = e.tensor_copy(xfm[dc][:, tt * 128:(tt + 1) * 128],
                                            pst[half][:, j * 128:(j + 1) * 128])
                    ins.then_inc(h, 1)
                ctx.v(ev)
        ctx.barrier()
        for dc in range(ND):
            h = _inc(dout, 16)
            ctx.g(lambda e, s=s, dc=dc, h=h:
                  e.dma_start(x_fm_d[s, dc], xfm[dc][:, :]).then_inc(h, 16))
            h = _inc(cast_d, 1)
            ctx.v(lambda e, dc=dc, h=h: e.tensor_copy(xb[dc][:, :], xfm[dc][:, :]).then_inc(h, 1))
        ctx.wait("t", cast_d)
        ctx.wait("s", cast_d)
        emit_ln_from_fm(ctx, "p", xfm, xb, xb, aux_sems, consts)
        ctx.wait("g", aux_sems["nrm"])
        for dc in range(ND):
            h = _inc(dout, 16)
            ctx.g(lambda e, s=s, dc=dc, h=h:
                  e.dma_start(xn0_d[s, dc], xb[dc][:, :]).then_inc(h, 16))
        ctx.barrier()
    for s in range(NSEQ):
        _seq_body0(s)
    ctx.wait("y", dout)
    st.close()



STAGES = {
    "prep": dict(fn=stage_prep, reads=["x", "ident_f32"], writes=["x_fm", "xn0"]),
}


_WSHAPES = None

def _weight_shapes():
    global _WSHAPES
    if _WSHAPES is None:
        dummy = {}
        specs = [
            ("L0_attn_w_qkv", (3072, 1024)), ("L0_attn_b_qkv", (3072,)),
            ("L0_attn_w_proj", (1024, 1024)), ("L0_attn_b_proj", (1024,)),
            ("L0_ffn_W1", (4096, 1024)), ("L0_ffn_b1", (4096,)),
            ("L0_ffn_W2", (1024, 4096)), ("L0_ffn_b2", (1024,)),
            ("L1_attn_ln_scale", (1024,)), ("L1_attn_ln_bias", (1024,)),
            ("L1_attn_W_hidden", (1024, 4096)), ("L1_attn_b_hidden", (4096,)),
            ("L1_attn_W_qk", (1024, 128)), ("L1_attn_b_qk", (128,)),
            ("L1_attn_os_weight", (2, 128)), ("L1_attn_os_bias", (2, 128)),
            ("L1_attn_W_out", (2048, 1024)), ("L1_attn_b_out", (1024,)),
            ("L1_ffn_gw1", (256, 1024)), ("L1_ffn_gb1", (256,)),
            ("L1_ffn_gw2", (8, 256)), ("L1_ffn_gb2", (8,)),
        ]
        for p in ("q", "k", "v", "o"):
            specs += [(f"L1_ffn_{p}Wm", (1024, 1024)), (f"L1_ffn_{p}U", (8, 1024, 16)),
                      (f"L1_ffn_{p}S", (8, 16)), (f"L1_ffn_{p}V", (8, 16, 1024)),
                      (f"L1_ffn_{p}b", (1024,))]
        for k, s in specs:
            dummy[k] = np.zeros(s, dtype=np.float32)
        _WSHAPES = {k: v.shape for k, v in prep_host(dummy).items()}
    return _WSHAPES


def _tensor_spec(name):
    if name in INTER:
        return INTER[name]
    if name in CORE_INPUTS:
        return CORE_INPUTS[name]
    ws = _weight_shapes()
    if name in ws:
        dt = BF16 if name == "ident_bf16" else F32
        return ws[name], dt
    raise KeyError(name)


def build_stage_program(names, final_outputs=None):
    global _CUR_SKIP_LDW
    nc = bass.Bass("TRN2", target_bir_lowering=False)
    _CUR_SKIP_LDW = set()
    _SKIP_LDW_BY_MODULE[id(nc)] = _CUR_SKIP_LDW
    produced = set()
    needed_in, needed_out = [], []
    for n in names:
        stg = STAGES[n]
        for r in stg["reads"]:
            if r not in produced and r not in needed_in:
                needed_in.append(r)
        for w_ in stg["writes"]:
            produced.add(w_)
            if w_ not in needed_out:
                needed_out.append(w_)
    io = {}
    for r in needed_in:
        shape, dt = _tensor_spec(r)
        io[r] = nc.dram_tensor(r, list(shape), dt, kind="ExternalInput").ap()
    if final_outputs is not None:
        needed_out = [w_ for w_ in needed_out if w_ in final_outputs]
        for w_ in list(produced):
            if w_ not in needed_out:
                shape, dt = _tensor_spec(w_)
                io[w_] = nc.dram_tensor(w_, list(shape), dt, kind="Internal").ap()
    for w_ in needed_out:
        shape, dt = _tensor_spec(w_)
        io[w_] = nc.dram_tensor(w_, list(shape), dt, kind="ExternalOutput").ap()
    ctx = Ctx(nc)
    for i, n in enumerate(names):
        STAGES[n]["fn"](ctx, io)
        if i + 1 < len(names):
            ctx.barrier()
            ctx.barrier()
    with nc.Block() as blk:
        ctx.build(blk)
    return nc, needed_in, needed_out



FUSED = True
if FUSED:
    STAGE_GROUPS = [["prep", "l0attn", "l0ffn", "ln2", "l1attn", "l1ffn"]]
else:
    STAGE_GROUPS = [["prep"], ["l0attn"], ["l0ffn"], ["ln2"], ["l1attn"], ["l1ffn"]]


def kernel(**inputs):
    host_w = prep_host(inputs)
    x_full = np.ascontiguousarray(np.asarray(inputs["L0_attn_x"]), dtype=np.float32)
    stores = []
    for c in range(NCORE):
        d = dict(host_w)
        d["x"] = np.ascontiguousarray(x_full[2 * c:2 * c + 2])
        stores.append(d)

    for grp in STAGE_GROUPS:
        nc, ins, outs = build_stage_program(grp, final_outputs={"out"} if FUSED else None)
        in_maps = [{k: np.ascontiguousarray(stores[c][k]) for k in ins} for c in range(NCORE)]
        res = run_bass_kernel_spmd(nc, in_maps, core_ids=list(range(NCORE)))
        for c in range(NCORE):
            for k in outs:
                stores[c][k] = res.results[c][k]

    out = np.concatenate([stores[c]["out"] for c in range(NCORE)], axis=0)
    return out.reshape(16, T, D).astype(np.float32)



def stage_l0attn(ctx, io):
    nc = ctx.nc
    xn0_d = io["xn0"]; s0_d = io["s0"]
    wqkv_d = io["w_qkvT"]; wproj_d = io["w_projT"]
    bqq_d = io["bq_q_fm"]; bqk_d = io["bq_k_fm"]; bqv_d = io["bq_v_row"]
    bproj_d = io["b_proj_fm"]; ident_d = io["ident_f32"]

    outer = ExitStack()
    E = outer.enter_context
    ctx_asm = [E(nc.sbuf_tensor(f"a_ctx{c}", [128, T], BF16)) for c in range(ND)]
    idf = E(nc.sbuf_tensor("a_idf", [128, 128], F32))
    ones_b = E(nc.sbuf_tensor("a_ones_b", [1, 128], BF16))
    ones_f = E(nc.sbuf_tensor("a_ones_f", [1, 128], F32))
    bqq = E(nc.sbuf_tensor("a_bqq", [128, 8], F32))
    bqk = E(nc.sbuf_tensor("a_bqk", [128, 8], F32))
    bqv = E(nc.sbuf_tensor("a_bqv", [1, 1024], BF16))
    bpr = E(nc.sbuf_tensor("a_bpr", [128, 8], F32))
    scq = E(nc.sbuf_tensor("a_scq", [128, 1], F32))

    psg = [E(nc.psum_tensor(f"a_psg{i}", [128, 1024], F32)) for i in range(4)]
    ps_ST = [psg[0], psg[1]]
    ps_S = [psg[2][:, 0:512], psg[2][:, 512:1024]]
    ps_bc = [psg[3][0:64, 0:256], psg[3][0:64, 512:768]]

    dw = ctx.sem("a_dw")
    mmqk = ctx.sem("a_mmqk")
    evqk = ctx.sem("a_evqk")
    smm = ctx.sem("a_smm")
    maxd = ctx.sem("a_maxd")
    strd = ctx.sem("a_strd")
    aux2 = ctx.sem("a_aux2")
    dreg = ctx.sem("a_dreg")
    stmm = ctx.sem("a_stmm")
    expd = ctx.sem("a_expd")
    ctxd = ctx.sem("a_ctxd")
    evc = ctx.sem("a_evc")
    dreg2 = ctx.sem("a_dreg2")
    aux3 = ctx.sem("a_aux3")
    dreg3 = ctx.sem("a_dreg3")
    bcd = ctx.sem("a_bcd")
    evb = ctx.sem("a_evb")
    dout = ctx.sem("a_dout")

    ctx.g(lambda e: e.memset(ones_b[:, :], 1.0))
    ctx.g(lambda e: e.memset(ones_f[:, :], 1.0))
    for dst, src_ in ((idf, ident_d), (bqq, bqq_d), (bqk, bqk_d), (bpr, bproj_d)):
        h = _inc(dw, 16)
        ctx.g(lambda e, dst=dst, src_=src_, h=h: e.dma_start(dst[:, :], src_).then_inc(h, 16))
    h = _inc(dw, 16)
    ctx.g(lambda e, h=h: e.dma_start(bqv[:, :], bqv_d).then_inc(h, 16))
    ctx.wait("g", dw)
    auxq = ctx.sem("a_auxq")
    hm = _inc(auxq, 1)
    ctx.g(lambda e, h=hm: e.memset(scq[:, :], 0.125).then_inc(h, 1))
    ctx.wait("v", dw)
    ctx.wait("v", auxq)
    h = _inc(auxq, 1)
    ctx.v(lambda e, h=h: e.tensor_scalar(bqq[:, :], bqq[:, :], scq[:, 0:1], None,
                                         ALU.mult).then_inc(h, 1))
    ctx.barrier()

    def _seq_body1(s):
        with ExitStack() as bc_stack:
            EBC = bc_stack.enter_context
            q_sb = [EBC(nc.sbuf_tensor(f"a_q{c}_{s}", [128, T], BF16)) for c in range(ND)]
            k_sb = [EBC(nc.sbuf_tensor(f"a_k{c}_{s}", [128, T], BF16)) for c in range(ND)]
            v_sb = [EBC(nc.sbuf_tensor(f"a_v{t}_{s}", [128, 16 * 65], BF16)) for t in range(NT)]
            for tt in range(NT):
                ctx.g(lambda e, tt=tt: e.memset(
                    v_sb[tt][:, :].rearrange("p (h c) -> p h c", c=65)[:, :, 64:65], 1.0))
            with ExitStack() as phB:
                EB = phB.enter_context
                xn_sb = [EB(nc.sbuf_tensor(f"a_xn{c}_{s}", [128, T], BF16)) for c in range(ND)]
                wq = [EB(nc.sbuf_tensor(f"a_wq{c}_{s}", [128, 1024], BF16)) for c in range(ND)]
                for dc in range(ND):
                    h = _inc(dw, 16)
                    ctx.g(lambda e, s=s, dc=dc, h=h:
                          e.dma_start(xn_sb[dc][:, :], xn0_d[s, dc]).then_inc(h, 16))
                ctx.wait("g", dw)
                for sec in range(3):
                    for dc in range(ND):
                        h = _inc(dw, 16)
                        ctx.g(lambda e, dc=dc, sec=sec, h=h:
                              e.dma_start(wq[dc][:, :],
                                          wqkv_d[dc * 128:(dc + 1) * 128,
                                                 sec * 1024:(sec + 1) * 1024]).then_inc(h, 16))
                    ctx.wait("g", dw)
                    ctx.wait("t", dw)
                    if sec < 2:
                        osb = q_sb if sec == 0 else k_sb
                        bcol = bqq if sec == 0 else bqk
                        for oc in range(ND):
                            for th in range(2):
                                grp = psg[(oc * 2 + th) % 4]
                                if evqk.v >= 4:
                                    ctx.wait("t", evqk, evqk.v - 3)
                                for dc in range(ND):
                                    st_ = (dc == 0); sp = (dc == ND - 1)
                                    hh = _inc(mmqk, 1) if sp else None
                                    def qkm(e, dc=dc, oc=oc, th=th, grp=grp, st_=st_, sp=sp, hh=hh):
                                        for tb in range(2):
                                            ins = e.matmul(grp[:, tb * 512:(tb + 1) * 512],
                                                           wq[dc][:, oc * 128:(oc + 1) * 128],
                                                           xn_sb[dc][:, th * 1024 + tb * 512: th * 1024 + (tb + 1) * 512],
                                                           start=st_, stop=sp)
                                            if tb > 0:
                                                _CUR_SKIP_LDW.add(ins.ins.name)
                                        if hh is not None:
                                            ins.then_inc(hh, 1)
                                    ctx.t(qkm)
                                ctx.wait("s", mmqk)
                                h = _inc(evqk, 1)
                                if sec == 0:
                                    ctx.s(lambda e, oc=oc, th=th, grp=grp, h=h, osb=osb, bcol=bcol:
                                          e.activation(osb[oc][:, th * 1024:(th + 1) * 1024], grp[:, :],
                                                       AF.Identity, bias=bcol[:, oc:oc + 1],
                                                       scale=scq[:, 0:1]).then_inc(h, 1))
                                else:
                                    ctx.s(lambda e, oc=oc, th=th, grp=grp, h=h, osb=osb, bcol=bcol:
                                          e.activation(osb[oc][:, th * 1024:(th + 1) * 1024], grp[:, :],
                                                       AF.Identity, bias=bcol[:, oc:oc + 1]).then_inc(h, 1))
                        ctx.wait("g", evqk)
                        ctx.wait("t", evqk)
                    else:
                        for tt in range(NT):
                            grp = psg[tt % 4]
                            if evc.v >= 4:
                                ctx.wait("t", evc, evc.v - 3)
                            for dc in range(ND):
                                st_ = (dc == 0)
                                def vmm(e, dc=dc, tt=tt, grp=grp, st_=st_):
                                    for vb in range(2):
                                        ins = e.matmul(grp[:, vb * 512:(vb + 1) * 512],
                                                 xn_sb[dc][:, tt * 128:(tt + 1) * 128],
                                                 wq[dc][:, vb * 512:(vb + 1) * 512],
                                                 start=st_, stop=False)
                                        if vb > 0:
                                            _CUR_SKIP_LDW.add(ins.ins.name)
                                ctx.t(vmm)
                            h = _inc(stmm, 1)
                            def vbias(e, tt=tt, grp=grp, h=h):
                                e.matmul(grp[:, 0:512], ones_b[:, :], bqv[0:1, 0:512],
                                         start=False, stop=True)
                                ins = e.matmul(grp[:, 512:1024], ones_b[:, :], bqv[0:1, 512:1024],
                                         start=False, stop=True)
                                _CUR_SKIP_LDW.add(ins.ins.name)
                                ins.then_inc(h, 1)
                            ctx.t(vbias)
                            ctx.wait("v", stmm)
                            h = _inc(evc, 1)
                            ctx.v(lambda e, tt=tt, grp=grp, h=h:
                                  e.tensor_copy(
                                      v_sb[tt][:, :].rearrange("p (h c) -> p h c", c=65)[:, :, 0:64],
                                      grp[:, :].rearrange("p (h c) -> p h c", c=64)).then_inc(h, 1))
            ctx.barrier()
            with ExitStack() as phC:
                EC = phC.enter_context
                statg = EC(nc.sbuf_tensor(f"a_statg_{s}", [128, 32], F32))
                stT_b = EC(nc.sbuf_tensor(f"a_stT_{s}", [32, 128], BF16))
                crow_b = EC(nc.sbuf_tensor(f"a_crow_{s}", [1, 4096], BF16))
                ctxden = EC(nc.sbuf_tensor(f"a_ctxden_{s}", [65, 4096], F32))
                dgrid = EC(nc.sbuf_tensor(f"a_dgrid_{s}", [128, 32], F32))
                rgr2 = EC(nc.sbuf_tensor(f"a_rgr2_{s}", [128, 32], F32))
                rrow2 = EC(nc.sbuf_tensor(f"a_rrow2_{s}", [1, 4096], F32))
                Pb = [EC(nc.sbuf_tensor(f"a_Pb{i}_{s}", [128, 1024], BF16)) for i in range(2)]
                exp_hist = []
                ev_hist = []
                for g in range(8):
                    if g > 0 or s > 0:
                        ctx.wait("t", evc)
                    kbase = 256 * g - 128
                    kcs = [kc for kc in range(4) if 0 <= kbase + 128 * kc and kbase + 128 * kc + 128 <= T]
                    L = len(kcs)
                    qsl = slice(g * 256, (g + 1) * 256)
                    col = 0
                    for hh in range(16):
                        hc = hh // 2; hr = 64 * (hh % 2)
                        kst = kbase + 128 * kcs[0]
                        nk = 128 * L
                        for qt in range(2):
                            par = col % 2
                            if maxd.v >= 2:
                                ctx.wait("t", maxd, maxd.v - 1)
                            h = _inc(smm, 1)
                            ctx.t(lambda e, hc=hc, hr=hr, g=g, qt=qt, kst=kst, nk=nk, par=par, h=h:
                                  e.matmul(ps_S[par][:, 0:nk],
                                           q_sb[hc][hr:hr + 64, g * 256 + qt * 128: g * 256 + (qt + 1) * 128],
                                           k_sb[hc][hr:hr + 64, kst:kst + nk],
                                           start=True, stop=True).then_inc(h, 1))
                            ctx.wait("v", smm)
                            h = _inc(maxd, 1)
                            ctx.v(lambda e, col=col, nk=nk, par=par, h=h:
                                  e.tensor_reduce(statg[:, col:col + 1], ps_S[par][:, 0:nk], AX.X,
                                                  ALU.max, negate=True).then_inc(h, 1))
                            col += 1
                    ctx.wait("t", maxd)
                    ctx.wait("t", expd)
                    h = _inc(strd, 1)
                    ctx.t(lambda e, h=h: e.transpose(ps_ST[0][0:32, 0:128], statg[:, :], idf[:, :]).then_inc(h, 1))
                    ctx.wait("v", strd)
                    ctx.wait("v", dreg)
                    h = _inc(aux2, 1)
                    ctx.v(lambda e, h=h: e.tensor_copy(stT_b[:, :], ps_ST[0][0:32, 0:128]).then_inc(h, 1))
                    ctx.wait("y", aux2)
                    h = _inc(dreg, 16)
                    ctx.y(lambda e, h=h: e.dma_start(crow_b[:, :], stT_b[:, :]).then_inc(h, 16))
                    ctx.wait("t", dreg)
                    exp_tgt = {}
                    def emit_stgroup(hh, kcs=kcs, kbase=kbase, qsl=qsl, L=L):
                        hc = hh // 2; hr = 64 * (hh % 2)
                        par = hh % 2
                        for idx, kc in enumerate(kcs):
                            kst = kbase + 128 * kc
                            ctx.t(lambda e, hc=hc, hr=hr, kst=kst, idx=idx, qsl=qsl, par=par:
                                  e.matmul(ps_ST[par][:, idx * 256:(idx + 1) * 256],
                                           k_sb[hc][hr:hr + 64, kst:kst + 128],
                                           q_sb[hc][hr:hr + 64, qsl],
                                           start=True, stop=False))
                            last = (idx == L - 1)
                            if last:
                                h = _inc(stmm, 1)
                                ctx.t(lambda e, hh=hh, idx=idx, par=par, h=h:
                                      e.matmul(ps_ST[par][:, idx * 256:(idx + 1) * 256], ones_b[:, :],
                                               crow_b[0:1, 256 * hh:256 * hh + 256],
                                               start=False, stop=True).then_inc(h, 1))
                            else:
                                ctx.t(lambda e, hh=hh, idx=idx, par=par:
                                      e.matmul(ps_ST[par][:, idx * 256:(idx + 1) * 256], ones_b[:, :],
                                               crow_b[0:1, 256 * hh:256 * hh + 256],
                                               start=False, stop=True))
                        ctx.wait("s", stmm)
                        h = _inc(expd, 1)
                        ctx.s(lambda e, par=par, L=L, h=h:
                              e.activation(Pb[par][:, 0:256 * L], ps_ST[par][:, 0:256 * L], AF.Exp).then_inc(h, 1))
                        exp_tgt[hh] = expd.v
                    emit_stgroup(0)
                    for hh in range(16):
                        hc = hh // 2; hr = 64 * (hh % 2)
                        par = hh % 2
                        if hh + 1 < 16:
                            emit_stgroup(hh + 1)
                        ctx.wait("t", expd, exp_tgt[hh])
                        if len(ev_hist) >= 2:
                            ctx.wait("t", evc, ev_hist[-2])
                        for idx, kc in enumerate(kcs):
                            tt = (kbase + 128 * kc) // 128
                            last = (idx == L - 1)
                            if last:
                                h = _inc(ctxd, 1)
                                ctx.t(lambda e, tt=tt, hh=hh, idx=idx, par=par, h=h, st_=(idx == 0):
                                      e.matmul(ps_S[par][0:65, 0:256],
                                               v_sb[tt][:, 65 * hh:65 * hh + 65],
                                               Pb[par][:, idx * 256:(idx + 1) * 256],
                                               start=st_, stop=True).then_inc(h, 1))
                            else:
                                ctx.t(lambda e, tt=tt, hh=hh, idx=idx, par=par, st_=(idx == 0):
                                      e.matmul(ps_S[par][0:65, 0:256],
                                               v_sb[tt][:, 65 * hh:65 * hh + 65],
                                               Pb[par][:, idx * 256:(idx + 1) * 256],
                                               start=st_, stop=False))
                        ctx.wait("v", ctxd)
                        if hh == 0:
                            ctx.wait("v", dreg2)
                        h = _inc(evc, 1)
                        ctx.v(lambda e, hh=hh, par=par, h=h:
                              e.tensor_copy(ctxden[:, 256 * hh:256 * hh + 256], ps_S[par][0:65, 0:256]).then_inc(h, 1))
                        ev_hist.append(evc.v)
                    ctx.wait("y", evc)
                    ctx.wait("y", aux3)
                    h = _inc(dreg2, 16)
                    ctx.y(lambda e, h=h: e.dma_start(dgrid[:, :], ctxden[64:65, :]).then_inc(h, 16))
                    ctx.wait("v", dreg2)
                    ctx.wait("v", dreg3)
                    h = _inc(aux3, 1)
                    ctx.v(lambda e, h=h: e.reciprocal(rgr2[:, :], dgrid[:, :]).then_inc(h, 1))
                    ctx.wait("y", aux3)
                    ctx.wait("y", bcd)
                    h = _inc(dreg3, 16)
                    ctx.y(lambda e, h=h: e.dma_start(rrow2[:, :], rgr2[:, :]).then_inc(h, 16))
                    ctx.wait("t", dreg3)
                    for hh in range(16):
                        par = hh % 2
                        if evb.v >= 2:
                            ctx.wait("t", evb, evb.v - 1)
                        h = _inc(bcd, 1)
                        ctx.t(lambda e, hh=hh, par=par, h=h:
                              e.matmul(ps_bc[par][:, :], ones_f[0:1, 0:64], rrow2[0:1, 256 * hh:256 * hh + 256],
                                       start=True, stop=True).then_inc(h, 1))
                        ctx.wait("v", bcd)
                        h = _inc(evb, 1)
                        hc = hh // 2; hr = 64 * (hh % 2)
                        ctx.v(lambda e, hh=hh, par=par, hc=hc, hr=hr, g=g, h=h:
                              e.tensor_mul(ctx_asm[hc][hr:hr + 64, g * 256:(g + 1) * 256],
                                           ctxden[0:64, 256 * hh:256 * hh + 256],
                                           ps_bc[par][:, :]).then_inc(h, 1))
            ctx.barrier()
        with ExitStack() as phD:
            ED = phD.enter_context
            wp = [ED(nc.sbuf_tensor(f"a_wp{c}_{s}", [128, 1024], BF16)) for c in range(ND)]
            s0_sb = [ED(nc.sbuf_tensor(f"a_s0{c}_{s}", [128, T], BF16)) for c in range(ND)]
            for dc in range(ND):
                h = _inc(dw, 16)
                ctx.g(lambda e, dc=dc, h=h:
                      e.dma_start(wp[dc][:, :], wproj_d[dc * 128:(dc + 1) * 128, :]).then_inc(h, 16))
            ctx.wait("g", dw)
            ctx.wait("t", dw)
            for oc in range(ND):
                for th in range(2):
                    grp = psg[(oc * 2 + th) % 4]
                    if evqk.v >= 4:
                        ctx.wait("t", evqk, evqk.v - 3)
                    for dc in range(ND):
                        st_ = (dc == 0); sp = (dc == ND - 1)
                        hh = _inc(mmqk, 1) if sp else None
                        def pjm(e, dc=dc, oc=oc, th=th, grp=grp, st_=st_, sp=sp, hh=hh):
                            for tb in range(2):
                                ins = e.matmul(grp[:, tb * 512:(tb + 1) * 512],
                                               wp[dc][:, oc * 128:(oc + 1) * 128],
                                               ctx_asm[dc][:, th * 1024 + tb * 512: th * 1024 + (tb + 1) * 512],
                                               start=st_, stop=sp)
                                if tb > 0:
                                    _CUR_SKIP_LDW.add(ins.ins.name)
                            if hh is not None:
                                ins.then_inc(hh, 1)
                        ctx.t(pjm)
                    ctx.wait("s", mmqk)
                    h = _inc(evqk, 1)
                    ctx.s(lambda e, oc=oc, th=th, grp=grp, h=h:
                          e.activation(s0_sb[oc][:, th * 1024:(th + 1) * 1024], grp[:, :],
                                       AF.Identity, bias=bpr[:, oc:oc + 1]).then_inc(h, 1))
            ctx.wait("g", evqk)
            for dc in range(ND):
                h = _inc(dout, 16)
                ctx.g(lambda e, s=s, dc=dc, h=h:
                      e.dma_start(s0_d[s, dc], s0_sb[dc][:, :]).then_inc(h, 16))
        ctx.barrier()
    for s in range(NSEQ):
        _seq_body1(s)
    ctx.wait("y", dout)
    outer.close()


STAGES["l0attn"] = dict(
    fn=stage_l0attn,
    reads=["xn0", "w_qkvT", "bq_q_fm", "bq_k_fm", "bq_v_row", "w_projT",
           "b_proj_fm", "ident_f32"],
    writes=["s0"],
)



def stage_l0ffn(ctx, io):
    nc = ctx.nc
    xn0_d = io["xn0"]; s0_d = io["s0"]; x_fm_d = io["x_fm"]; x1_d = io["x1_fm"]
    W1T_d = io["W1T"]; W2T_d = io["W2T"]
    b1_d = io["b1_fm"]; b2_d = io["b2_fm"]
    wq1_d = nc.dram_tensor("Wq1T_i", [1024, 4096], BF16, kind="Internal").ap()
    wq2_d = nc.dram_tensor("Wq2T_i", [4096, 1024], BF16, kind="Internal").ap()

    outer = ExitStack()
    E = outer.enter_context
    ones_col_f = E(nc.sbuf_tensor("f_ones_col_f", [128, 1], F32))
    ones_row_f = E(nc.sbuf_tensor("f_ones_row_f", [1, 128], F32))
    cpool = E(nc.sbuf_tensor("f_cpool", [128, 4], F32))
    b1 = E(nc.sbuf_tensor("f_b1", [128, 32], F32))
    b2 = E(nc.sbuf_tensor("f_b2", [128, 8], F32))
    psg = [E(nc.psum_tensor(f"f_psg{i}", [128, 1024], F32)) for i in range(4)]

    dw = ctx.sem("f_dw")
    tld = [ctx.sem("f_tld0"), ctx.sem("f_tld1")]
    stx = [ctx.sem("f_stx0"), ctx.sem("f_stx1")]
    mmd = ctx.sem("f_mmd")
    evd = ctx.sem("f_evd")
    aux = ctx.sem("f_aux")
    dout = ctx.sem("f_dout")

    ctx.g(lambda e: e.memset(ones_col_f[:, :], 1.0))
    ctx.g(lambda e: e.memset(ones_row_f[:, :], 1.0))
    ctx.g(lambda e: e.memset(cpool[:, 0:1], -1.0))
    for dst, src_ in ((b1, b1_d), (b2, b2_d)):
        h = _inc(dw, 16)
        ctx.g(lambda e, dst=dst, src_=src_, h=h: e.dma_start(dst[:, :], src_).then_inc(h, 16))
    ctx.wait("g", dw)
    ctx.barrier()

    with ExitStack() as phQ:
        EQ = phQ.enter_context
        wbuf = [EQ(nc.sbuf_tensor(f"f_wb{i}", [128, 4096], F32)) for i in range(2)]
        cbuf = EQ(nc.sbuf_tensor("f_cb", [128, 4096], F32))
        ubuf = EQ(nc.sbuf_tensor("f_ub", [128, 4096], F32))
        wqb = EQ(nc.sbuf_tensor("f_wqb", [128, 4096], BF16))
        abscol = EQ(nc.sbuf_tensor("f_abscol", [128, 40], F32))
        trow = EQ(nc.sbuf_tensor("f_trow", [1, 4], F32))
        tcol = EQ(nc.sbuf_tensor("f_tcol", [128, 2], F32))
        ntcol = EQ(nc.sbuf_tensor("f_ntcol", [128, 2], F32))
        ps_t = psg[0]
        red = ctx.sem("f_red")
        qld = [ctx.sem("f_qld0"), ctx.sem("f_qld1")]

        chunks = [("w1", dc, W1T_d[dc * 128:(dc + 1) * 128, :], 4096, dc)
                  for dc in range(ND)] + \
                 [("w2", fc, W2T_d[fc * 128:(fc + 1) * 128, :], 1024, 8 + fc)
                  for fc in range(32)]
        for i, (tag, idx, src_, width, col) in enumerate(chunks):
            par = i % 2
            if i >= 2:
                ctx.wait("g", red, red.v - 1)
            h = _inc(qld[par], 16)
            ctx.g(lambda e, par=par, src_=src_, width=width, h=h:
                  e.dma_start(wbuf[par][:, 0:width], src_).then_inc(h, 16))
            ctx.wait("v", qld[par])
            h = _inc(red, 1)
            ctx.v(lambda e, par=par, width=width, col=col, h=h:
                  e.tensor_reduce(abscol[:, col:col + 1], wbuf[par][:, 0:width], AX.X,
                                  ALU.add, apply_absolute_value=True).then_inc(h, 1))
        ctx.wait("t", red)
        h = _inc(mmd, 1)
        def tot(e, h=h):
            e.matmul(ps_t[0:1, 0:8], ones_col_f[:, :], abscol[:, 0:8], start=True, stop=True)
            e.matmul(ps_t[0:1, 0:8], ones_col_f[:, :], abscol[:, 8:16], start=False, stop=False)
            e.matmul(ps_t[0:1, 0:8], ones_col_f[:, :], abscol[:, 16:24], start=False, stop=False)
            e.matmul(ps_t[0:1, 0:8], ones_col_f[:, :], abscol[:, 24:32], start=False, stop=False)
            e.matmul(ps_t[0:1, 0:8], ones_col_f[:, :], abscol[:, 32:40], start=False, stop=True).then_inc(h, 1)
        ctx.t(tot)
        mmd.v -= 1
        ctx.q["t"].pop()
        h = _inc(mmd, 1)
        def tot2(e, h=h):
            e.matmul(ps_t[0:1, 0:8], ones_col_f[:, :], abscol[:, 0:8], start=True, stop=True)
            e.matmul(ps_t[0:1, 8:16], ones_col_f[:, :], abscol[:, 8:16], start=True, stop=False)
            e.matmul(ps_t[0:1, 8:16], ones_col_f[:, :], abscol[:, 16:24], start=False, stop=False)
            e.matmul(ps_t[0:1, 8:16], ones_col_f[:, :], abscol[:, 24:32], start=False, stop=False)
            e.matmul(ps_t[0:1, 8:16], ones_col_f[:, :], abscol[:, 32:40], start=False, stop=True).then_inc(h, 1)
        ctx.t(tot2)
        ctx.wait("v", mmd)
        h = _inc(aux, 1)
        def thr(e, h=h):
            e.tensor_reduce(trow[0:1, 0:1], ps_t[0:1, 0:8], AX.X, ALU.add)
            e.tensor_reduce(trow[0:1, 1:2], ps_t[0:1, 8:16], AX.X, ALU.add).then_inc(h, 1)
        ctx.v(thr)
        ctx.wait("v", aux)
        h = _inc(aux, 1)
        def thr2(e, h=h):
            e.tensor_scalar(trow[0:1, 2:3], trow[0:1, 0:1], 0.5 / (1024.0 * 4096.0), 0.5e-5,
                            ALU.mult, ALU.add)
            e.tensor_scalar(trow[0:1, 3:4], trow[0:1, 1:2], 0.5 / (1024.0 * 4096.0), 0.5e-5,
                            ALU.mult, ALU.add).then_inc(h, 1)
        ctx.v(thr2)
        ctx.wait("t", aux)
        h = _inc(mmd, 1)
        ctx.t(lambda e, h=h: e.matmul(ps_t[:, 0:2], ones_row_f[:, :], trow[0:1, 2:4],
                                      start=True, stop=True).then_inc(h, 1))
        ctx.wait("v", mmd)
        h = _inc(aux, 1)
        def thr3(e, h=h):
            e.tensor_copy(tcol[:, :], ps_t[:, 0:2])
            e.tensor_scalar(ntcol[:, :], ps_t[:, 0:2], cpool[:, 0:1], None, ALU.mult).then_inc(h, 1)
        ctx.v(thr3)
        ctx.barrier()
        qi = 0
        for i, (tag, idx, src_, width, col) in enumerate(chunks):
            par = i % 2
            if i >= 2:
                ctx.wait("g", aux, aux.v - 1)
            h = _inc(qld[par], 16)
            ctx.g(lambda e, par=par, src_=src_, width=width, h=h:
                  e.dma_start(wbuf[par][:, 0:width], src_).then_inc(h, 16))
            ctx.wait("v", qld[par])
            tc = 0 if tag == "w1" else 1
            cw0 = aux.v
            cw1 = _inc(aux, 1) and aux.v
            _inc(aux, 1)
            h = aux.h
            def quant(e, par=par, width=width, tc=tc, h=h, cw0=cw0, cw1=cw1):
                if cw0 > 0:
                    e.wait_ge(h, cw0)
                e.tensor_scalar(cbuf[:, 0:width], wbuf[par][:, 0:width],
                                tcol[:, tc:tc + 1], ntcol[:, tc:tc + 1], ALU.min, ALU.max).then_inc(h, 1)
                e.wait_ge(h, cw1)
                e.tensor_sub(ubuf[:, 0:width], wbuf[par][:, 0:width], cbuf[:, 0:width]).then_inc(h, 1)
            ctx.v(quant)
            ctx.wait("s", aux)
            h = _inc(evd, 1)
            ctx.s(lambda e, width=width, h=h:
                  e.activation(wqb[:, 0:width], ubuf[:, 0:width], AF.Sign).then_inc(h, 1))
            ctx.wait("y", evd)
            dst = wq1_d[idx * 128:(idx + 1) * 128, :] if tag == "w1" else wq2_d[idx * 128:(idx + 1) * 128, :]
            h = _inc(dout, 16)
            ctx.y(lambda e, dst=dst, width=width, h=h:
                  e.dma_start(dst, wqb[:, 0:width]).then_inc(h, 16))
            ctx.wait("v", dout)
        ctx.barrier()

    def _seq_body2(s):
        with ExitStack() as phF:
            EF = phF.enter_context
            xn_sb = [EF(nc.sbuf_tensor(f"f_xn{c}_{s}", [128, T], BF16)) for c in range(ND)]
            h_sb = [EF(nc.sbuf_tensor(f"f_h{fc}_{s}", [128, 1024], BF16)) for fc in range(32)]
            w_flat = EF(nc.sbuf_tensor(f"f_wf_{s}", [128, 32768], BF16))
            gbuf = [EF(nc.sbuf_tensor(f"f_g{i}_{s}", [128, 1024], BF16)) for i in range(2)]
            s0t = [EF(nc.sbuf_tensor(f"f_s0t{i}_{s}", [128, 1024], BF16)) for i in range(2)]
            xt = [EF(nc.sbuf_tensor(f"f_xt{i}_{s}", [128, 1024], F32)) for i in range(2)]
            mbuf = EF(nc.sbuf_tensor(f"f_m_{s}", [128, 1024], F32))
            x1buf = [EF(nc.sbuf_tensor(f"f_x1{i}_{s}", [128, 1024], F32)) for i in range(2)]
            for dc in range(ND):
                h = _inc(dw, 16)
                ctx.g(lambda e, s=s, dc=dc, h=h:
                      e.dma_start(xn_sb[dc][:, :], xn0_d[s, dc]).then_inc(h, 16))
            ctx.wait("g", dw)
            for tblock in range(2):
                t0 = tblock * 1024
                ctx.wait("g", mmd)
                for dc in range(ND):
                    h = _inc(dw, 16)
                    ctx.g(lambda e, dc=dc, h=h:
                          e.dma_start(w_flat[:, dc * 4096:(dc + 1) * 4096],
                                      wq1_d[dc * 128:(dc + 1) * 128, :]).then_inc(h, 16))
                ctx.wait("g", dw)
                ctx.wait("t", dw)
                for fc in range(32):
                    grp = psg[fc % 4]
                    if evd.v >= 4:
                        ctx.wait("t", evd, evd.v - 3)
                    for dc in range(ND):
                        st_ = (dc == 0); sp = (dc == ND - 1)
                        hh = _inc(mmd, 1) if sp else None
                        def f1(e, dc=dc, fc=fc, t0=t0, grp=grp, st_=st_, sp=sp, hh=hh):
                            for tb in range(2):
                                ins = e.matmul(grp[:, tb * 512:(tb + 1) * 512],
                                               w_flat[:, dc * 4096 + fc * 128: dc * 4096 + (fc + 1) * 128],
                                               xn_sb[dc][:, t0 + tb * 512: t0 + (tb + 1) * 512],
                                               start=st_, stop=sp)
                                if tb > 0:
                                    _CUR_SKIP_LDW.add(ins.ins.name)
                            if hh is not None:
                                ins.then_inc(hh, 1)
                        ctx.t(f1)
                    ctx.wait("s", mmd)
                    h = _inc(evd, 1)
                    ctx.s(lambda e, fc=fc, grp=grp, h=h:
                          e.activation(h_sb[fc][:, :], grp[:, :], AF.Relu,
                                       bias=b1[:, fc:fc + 1]).then_inc(h, 1))
                ctx.wait("g", mmd)
                ctx.wait("g", evd)
                for fc in range(32):
                    h = _inc(dw, 16)
                    ctx.g(lambda e, fc=fc, h=h:
                          e.dma_start(w_flat[:, fc * 1024:(fc + 1) * 1024],
                                      wq2_d[fc * 128:(fc + 1) * 128, :]).then_inc(h, 16))
                ctx.wait("g", dw)
                ctx.wait("t", dw)
                for dc in range(ND):
                    grp = psg[dc % 4]
                    par = dc % 2
                    if evd.v >= 4:
                        ctx.wait("t", evd, evd.v - 3)
                    for fc in range(32):
                        st_ = (fc == 0); sp = (fc == 31)
                        hh = _inc(mmd, 1) if sp else None
                        def f2(e, dc=dc, fc=fc, grp=grp, st_=st_, sp=sp, hh=hh):
                            for tb in range(2):
                                ins = e.matmul(grp[:, tb * 512:(tb + 1) * 512],
                                               w_flat[:, fc * 1024 + dc * 128: fc * 1024 + (dc + 1) * 128],
                                               h_sb[fc][:, tb * 512:(tb + 1) * 512],
                                               start=st_, stop=sp)
                                if tb > 0:
                                    _CUR_SKIP_LDW.add(ins.ins.name)
                            if hh is not None:
                                ins.then_inc(hh, 1)
                        ctx.t(f2)
                    if aux.v >= 2:
                        ctx.wait("g", aux, aux.v - 1)
                    h = _inc(tld[par], 16)
                    ctx.g(lambda e, s=s, dc=dc, t0=t0, par=par, h=h:
                          e.dma_start(s0t[par][:, :], s0_d[s, dc, :, t0:t0 + 1024]).then_inc(h, 16))
                    h = _inc(tld[par], 16)
                    ctx.g(lambda e, s=s, dc=dc, t0=t0, par=par, h=h:
                          e.dma_start(xt[par][:, :], x_fm_d[s, dc, :, t0:t0 + 1024]).then_inc(h, 16))
                    if aux.v >= 2:
                        ctx.wait("s", aux, aux.v - 1)
                    ctx.wait("s", mmd)
                    h = _inc(evd, 1)
                    ctx.s(lambda e, dc=dc, grp=grp, par=par, h=h:
                          e.activation(gbuf[par][:, :], grp[:, :], AF.Silu,
                                       bias=b2[:, dc:dc + 1]).then_inc(h, 1))
                    ctx.wait("v", evd)
                    ctx.wait("v", tld[par])
                    ctx.wait("v", stx[par])
                    cw0 = aux.v
                    cw1 = _inc(aux, 1) and aux.v
                    _inc(aux, 1)
                    h = aux.h
                    def comb(e, dc=dc, par=par, h=h, cw0=cw0, cw1=cw1):
                        if cw0 > 0:
                            e.wait_ge(h, cw0)
                        e.tensor_mul(mbuf[:, :], gbuf[par][:, :], s0t[par][:, :]).then_inc(h, 1)
                        e.wait_ge(h, cw1)
                        e.tensor_add(x1buf[par][:, :], xt[par][:, :], mbuf[:, :]).then_inc(h, 1)
                    ctx.v(comb)
                    ctx.wait("y", aux)
                    h = _inc(stx[par], 16)
                    ctx.y(lambda e, s=s, dc=dc, t0=t0, par=par, h=h:
                          e.dma_start(x1_d[s, dc, :, t0:t0 + 1024], x1buf[par][:, :]).then_inc(h, 16))
        ctx.barrier()
    for s in range(NSEQ):
        _seq_body2(s)
    ctx.wait("y", dout)
    ctx.wait("y", stx[0])
    ctx.wait("y", stx[1])
    outer.close()


STAGES["l0ffn"] = dict(
    fn=stage_l0ffn,
    reads=["xn0", "s0", "x_fm", "W1T", "W2T", "b1_fm", "b2_fm"],
    writes=["x1_fm"],
)



def stage_ln2(ctx, io):
    nc = ctx.nc
    x1_d = io["x1_fm"]; xn1_d = io["xn1"]; pooled_d = io["pooled"]
    st = ExitStack()
    E = st.enter_context
    xfm = [E(nc.sbuf_tensor(f"n_xfm{c}", [128, T], F32)) for c in range(ND)]
    xb = [E(nc.sbuf_tensor(f"n_xb{c}", [128, T], BF16)) for c in range(ND)]
    x2 = [E(nc.sbuf_tensor(f"n_x2{i}", [128, 512], BF16)) for i in range(2)]
    ones_col_b = E(nc.sbuf_tensor("n_ones_col_b", [128, 1], BF16))
    ones_row_f = E(nc.sbuf_tensor("n_ones_row_f", [1, 128], F32))
    srow = E(nc.sbuf_tensor("n_srow", [1, T], F32))
    qrow = E(nc.sbuf_tensor("n_qrow", [1, T], F32))
    sgrid = E(nc.sbuf_tensor("n_sgrid", [128, 16], F32))
    qgrid = E(nc.sbuf_tensor("n_qgrid", [128, 16], F32))
    m2g = E(nc.sbuf_tensor("n_m2g", [128, 16], F32))
    varg = E(nc.sbuf_tensor("n_varg", [128, 16], F32))
    sqg = E(nc.sbuf_tensor("n_sqg", [128, 16], F32))
    rgrid = E(nc.sbuf_tensor("n_rg", [128, 16], F32))
    murow = E(nc.sbuf_tensor("n_murow", [1, T], F32))
    rrow = E(nc.sbuf_tensor("n_rrow", [1, T], F32))
    tmp = E(nc.sbuf_tensor("n_tmp", [128, 512], F32))
    tmpf = E(nc.sbuf_tensor("n_tmpf", [128, 512], F32))
    pcols = E(nc.sbuf_tensor("n_pcols", [128, 32], F32))
    pool = E(nc.sbuf_tensor("n_pool", [128, 8], F32))
    cpool = E(nc.sbuf_tensor("n_cpool", [128, 4], F32))
    ps_s = E(nc.psum_tensor("n_ps_s", [1, 512], F32))
    ps_q = E(nc.psum_tensor("n_ps_q", [1, 512], F32))
    ps_mu = E(nc.psum_tensor("n_ps_mu", [128, 512], F32))
    ps_r = E(nc.psum_tensor("n_ps_r", [128, 512], F32))

    din = ctx.sem("n_din")
    cast_d = ctx.sem("n_cast")
    dout = ctx.sem("n_dout")
    aux_sems = {k: ctx.sem("n_" + k) for k in ("sq", "smm", "aux", "dreg", "bc", "nrm")}
    pden = ctx.sem("n_pden")

    ctx.g(lambda e: e.memset(ones_col_b[:, :], 1.0))
    ctx.g(lambda e: e.memset(ones_row_f[:, :], 1.0))
    ctx.g(lambda e: e.memset(cpool[:, 0:1], 1.0 / 1024.0))
    ctx.g(lambda e: e.memset(cpool[:, 1:2], 1e-5))
    ctx.g(lambda e: e.memset(cpool[:, 2:3], 1.0 / 2048.0))
    ctx.barrier()

    consts = (ones_col_b, ones_row_f, cpool, srow, qrow, sgrid, qgrid, m2g, varg,
              sqg, rgrid, murow, rrow, tmp, x2, ps_s, ps_q, ps_mu, ps_r)

    def _seq_body3(s):
        for dc in range(ND):
            h = _inc(din, 16)
            ctx.g(lambda e, s=s, dc=dc, h=h:
                  e.dma_start(xfm[dc][:, :], x1_d[s, dc]).then_inc(h, 16))
        ctx.wait("g", din)
        ctx.wait("v", din)
        for dc in range(ND):
            h = _inc(cast_d, 1)
            ctx.v(lambda e, dc=dc, h=h: e.tensor_copy(xb[dc][:, :], xfm[dc][:, :]).then_inc(h, 1))
        ctx.wait("t", cast_d)
        ctx.wait("s", cast_d)
        emit_ln_stats(ctx, xb, aux_sems, consts)
        nrm_d = aux_sems["nrm"]; bc_d = aux_sems["bc"]; dreg = aux_sems["dreg"]
        ctx.wait("t", dreg)
        for tb in range(4):
            sl = slice(tb * 512, (tb + 1) * 512)
            if tb >= 1:
                ctx.wait("t", nrm_d)
            h = _inc(bc_d, 1)
            def bc(e, sl=sl, h=h):
                e.matmul(ps_mu[:, :], ones_row_f[:, :], murow[0:1, sl], start=True, stop=True)
                e.matmul(ps_r[:, :], ones_row_f[:, :], rrow[0:1, sl], start=True, stop=True).then_inc(h, 1)
            ctx.t(bc)
            ctx.wait("v", bc_d)
            for dc in range(ND):
                cw0 = nrm_d.v
                cw1 = _inc(nrm_d, 1) and nrm_d.v
                cw2 = _inc(nrm_d, 1) and nrm_d.v
                _inc(nrm_d, 1)
                h = nrm_d.h
                def nrm(e, dc=dc, sl=sl, tb=tb, h=h, cw0=cw0, cw1=cw1, cw2=cw2):
                    if cw0 > 0:
                        e.wait_ge(h, cw0)
                    e.tensor_sub(tmp[:, :], xfm[dc][:, sl], ps_mu[:, :]).then_inc(h, 1)
                    e.wait_ge(h, cw1)
                    e.tensor_mul(tmpf[:, :], tmp[:, :], ps_r[:, :]).then_inc(h, 1)
                    e.wait_ge(h, cw2)
                    e.tensor_copy(xb[dc][:, sl], tmpf[:, :])
                    e.tensor_reduce(pcols[:, dc * 4 + tb: dc * 4 + tb + 1], tmpf[:, :],
                                    AX.X, ALU.add).then_inc(h, 1)
                ctx.v(nrm)
        ctx.wait("v", nrm_d)
        h = _inc(pden, 1)
        def pfin(e, h=h):
            for dc in range(ND):
                ins = e.tensor_reduce(pool[:, dc:dc + 1],
                                      pcols[:, dc * 4:(dc + 1) * 4], AX.X, ALU.add)
            ins.then_inc(h, 1)
        ctx.v(pfin)
        ctx.wait("v", pden)
        h = _inc(pden, 1)
        ctx.v(lambda e, h=h: e.tensor_scalar(pool[:, :], pool[:, :], cpool[:, 2:3], None,
                                             ALU.mult).then_inc(h, 1))
        ctx.wait("g", pden)
        for dc in range(ND):
            h = _inc(dout, 16)
            ctx.g(lambda e, s=s, dc=dc, h=h:
                  e.dma_start(xn1_d[s, dc], xb[dc][:, :]).then_inc(h, 16))
            h = _inc(dout, 16)
            ctx.g(lambda e, s=s, dc=dc, h=h:
                  e.dma_start(pooled_d[s, dc], pool[:, dc:dc + 1]).then_inc(h, 16))
        ctx.barrier()
    for s in range(NSEQ):
        _seq_body3(s)
    ctx.wait("y", dout)
    st.close()


def emit_ln_stats(ctx, xb, aux_sems, consts):
    (ones_col_b, ones_row_f, cpool, srow, qrow, sgrid, qgrid, m2g, varg,
     sqg, rgrid, murow, rrow, tmp, x2, ps_s, ps_q, ps_mu, ps_r) = consts
    sq_d = aux_sems["sq"]; stat_mm = aux_sems["smm"]; aux = aux_sems["aux"]
    dreg = aux_sems["dreg"]
    qmm_hist = []
    for tb in range(4):
        sl = slice(tb * 512, (tb + 1) * 512)
        if tb >= 1:
            ctx.wait("t", aux)
        for dc in range(ND):
            p = dc % 2
            if len(qmm_hist) >= 2:
                ctx.wait("s", stat_mm, qmm_hist[-2])
            h = _inc(sq_d, 1)
            ctx.s(lambda e, dc=dc, sl=sl, p=p, h=h:
                  e.activation(x2[p][:, :], xb[dc][:, sl], AF.Square).then_inc(h, 1))
            start = (dc == 0); stop = (dc == ND - 1)
            h1 = _inc(stat_mm, 1)
            ctx.t(lambda e, dc=dc, sl=sl, start=start, stop=stop, h=h1:
                  e.matmul(ps_s[:, :], ones_col_b[:, :], xb[dc][:, sl],
                           start=start, stop=stop).then_inc(h, 1))
            ctx.wait("t", sq_d)
            h2 = _inc(stat_mm, 1)
            ctx.t(lambda e, p=p, start=start, stop=stop, h=h2:
                  e.matmul(ps_q[:, :], ones_col_b[:, :], x2[p][:, :],
                           start=start, stop=stop).then_inc(h, 1))
            qmm_hist.append(stat_mm.v)
        ctx.wait("v", stat_mm)
        h = _inc(aux, 1)
        def cpr(e, sl=sl, h=h):
            e.tensor_copy(srow[0:1, sl], ps_s[:, :])
            e.tensor_copy(qrow[0:1, sl], ps_q[:, :]).then_inc(h, 1)
        ctx.v(cpr)
    ctx.wait("y", aux)
    h = _inc(dreg, 16)
    ctx.y(lambda e, h=h: e.dma_start(sgrid[:, :], srow[:, :]).then_inc(h, 16))
    h = _inc(dreg, 16)
    ctx.y(lambda e, h=h: e.dma_start(qgrid[:, :], qrow[:, :]).then_inc(h, 16))
    ctx.wait("v", dreg)
    h = _inc(aux, 1)
    ctx.v(lambda e, h=h: e.tensor_scalar(sgrid[:, :], sgrid[:, :], cpool[:, 0:1], None,
                                         ALU.mult).then_inc(h, 1))
    ctx.wait("v", aux)
    h = _inc(aux, 1)
    ctx.v(lambda e, h=h: e.tensor_mul(m2g[:, :], sgrid[:, :], sgrid[:, :]).then_inc(h, 1))
    ctx.wait("v", aux)
    h = _inc(aux, 1)
    ctx.v(lambda e, h=h: e.scalar_tensor_tensor(varg[:, :], qgrid[:, :], cpool[:, 0:1],
                                                m2g[:, :], ALU.mult, ALU.subtract).then_inc(h, 1))
    ctx.wait("v", aux)
    h = _inc(aux, 1)
    ctx.v(lambda e, h=h: e.tensor_scalar(varg[:, :], varg[:, :], cpool[:, 1:2], None,
                                         ALU.add).then_inc(h, 1))
    ctx.wait("s", aux)
    h = _inc(aux, 1)
    ctx.s(lambda e, h=h: e.activation(sqg[:, :], varg[:, :], AF.Sqrt).then_inc(h, 1))
    ctx.wait("v", aux)
    h = _inc(aux, 1)
    ctx.v(lambda e, h=h: e.reciprocal(rgrid[:, :], sqg[:, :]).then_inc(h, 1))
    ctx.wait("y", aux)
    h = _inc(dreg, 16)
    ctx.y(lambda e, h=h: e.dma_start(murow[:, :], sgrid[:, :]).then_inc(h, 16))
    h = _inc(dreg, 16)
    ctx.y(lambda e, h=h: e.dma_start(rrow[:, :], rgrid[:, :]).then_inc(h, 16))


STAGES["ln2"] = dict(
    fn=stage_ln2,
    reads=["x1_fm"],
    writes=["xn1", "pooled"],
)



def stage_l1attn(ctx, io):
    nc = ctx.nc
    xn1_d = io["xn1"]; s1_d = io["s1"]
    whid_d = io["W_hidden"]; wqk_d = io["W_qk"]; wout_d = io["W_out"]
    lns_d = io["ln_scale_fm"]; lnb_d = io["ln_bias_fm"]
    bqk_d = io["b_qk_col"]; bhv_d = io["bh_v_row"]; bhg_d = io["bh_g_fm"]
    bout_d = io["b_out_fm"]
    osw0_d = io["os_w0"]; osb0_d = io["os_b0"]; osw1_d = io["os_w1"]; osb1_d = io["os_b1"]

    outer = ExitStack()
    E = outer.enter_context
    ones_col_b = E(nc.sbuf_tensor("g_ones_col_b", [128, 1], BF16))
    ones_row_f = E(nc.sbuf_tensor("g_ones_row_f", [1, 128], F32))
    ones_row_b = E(nc.sbuf_tensor("g_ones_row_b", [1, 128], BF16))
    cpool = E(nc.sbuf_tensor("g_cpool", [128, 4], F32))
    lns = E(nc.sbuf_tensor("g_lns", [128, 8], F32))
    lnb = E(nc.sbuf_tensor("g_lnb", [128, 8], F32))
    bqk = E(nc.sbuf_tensor("g_bqk", [128, 1], F32))
    vrow_b = E(nc.sbuf_tensor("g_vrow", [1, 2048], BF16))
    bhg = E(nc.sbuf_tensor("g_bhg", [128, 16], F32))
    bout = E(nc.sbuf_tensor("g_bout", [128, 8], F32))
    osc = E(nc.sbuf_tensor("g_osc", [128, 4], F32))

    psg = [E(nc.psum_tensor(f"g_psg{i}", [128, 1024], F32)) for i in range(4)]
    ps8 = [psg[i][:, j * 512:(j + 1) * 512] for i in range(4) for j in range(2)]
    ps_a, ps_b, ps_c, ps_d = ps8[0], ps8[1], ps8[6], ps8[7]
    ps_s = psg[1][0:1, 0:512]
    ps_q = psg[1][0:1, 512:1024]

    dw = ctx.sem("g_dw")
    mmd = ctx.sem("g_mmd")
    evd = ctx.sem("g_evd")
    aux = ctx.sem("g_aux")
    relu_d = ctx.sem("g_relu")
    sq2_d = ctx.sem("g_sq2")
    ogd = ctx.sem("g_ogd")
    dout = ctx.sem("g_dout")
    aux_sems = {k: ctx.sem("g_" + k) for k in ("sq", "smm", "aux", "dreg", "bc", "nrm")}

    ctx.g(lambda e: e.memset(ones_col_b[:, :], 1.0))
    ctx.g(lambda e: e.memset(ones_row_f[:, :], 1.0))
    ctx.g(lambda e: e.memset(ones_row_b[:, :], 1.0))
    ctx.g(lambda e: e.memset(cpool[:, 0:1], 1.0 / 1024.0))
    ctx.g(lambda e: e.memset(cpool[:, 1:2], 1e-5))
    ctx.g(lambda e: e.memset(cpool[:, 2:3], 1.0 / 2048.0))
    for dst, src_ in ((lns, lns_d), (lnb, lnb_d), (bqk, bqk_d), (bhg, bhg_d), (bout, bout_d),
                      (osc[:, 0:1], osw0_d), (osc[:, 1:2], osb0_d),
                      (osc[:, 2:3], osw1_d), (osc[:, 3:4], osb1_d)):
        h = _inc(dw, 16)
        ctx.g(lambda e, dst=dst, src_=src_, h=h: e.dma_start(dst if dst.__class__.__name__ != "SBTensorHandle" else dst[:, :], src_).then_inc(h, 16))
    h = _inc(dw, 16)
    ctx.g(lambda e, h=h: e.dma_start(vrow_b[:, :], bhv_d).then_inc(h, 16))
    ctx.wait("g", dw)
    ctx.wait("v", dw)
    h = _inc(aux, 1)
    def osfold(e, h=h):
        e.tensor_scalar(osc[:, 0:1], osc[:, 0:1], cpool[:, 2:3], None, ALU.mult)
        e.tensor_scalar(osc[:, 1:2], osc[:, 1:2], cpool[:, 2:3], None, ALU.mult).then_inc(h, 1)
    ctx.v(osfold)
    ctx.barrier()

    def _seq_body4(s):
        with ExitStack() as seqst:
            ES = seqst.enter_context
            xni = [ES(nc.sbuf_tensor(f"g_xni{c}_{s}", [128, T], BF16)) for c in range(ND)]
            v_sb = [ES(nc.sbuf_tensor(f"g_v{t}_{s}", [128, 2048], BF16)) for t in range(NT)]
            qp = ES(nc.sbuf_tensor(f"g_qp_{s}", [128, T], BF16))
            kp = ES(nc.sbuf_tensor(f"g_kp_{s}", [128, T], BF16))
            s1_sb = [ES(nc.sbuf_tensor(f"g_s1{c}_{s}", [128, T], BF16)) for c in range(ND)]
            with ExitStack() as phL:
                EL = phL.enter_context
                xn_sb = [EL(nc.sbuf_tensor(f"g_xn{c}_{s}", [128, T], BF16)) for c in range(ND)]
                srow = EL(nc.sbuf_tensor(f"g_srow_{s}", [1, T], F32))
                qrow = EL(nc.sbuf_tensor(f"g_qrow_{s}", [1, T], F32))
                sgrid = EL(nc.sbuf_tensor(f"g_sgrid_{s}", [128, 16], F32))
                qgrid = EL(nc.sbuf_tensor(f"g_qgrid_{s}", [128, 16], F32))
                m2g = EL(nc.sbuf_tensor(f"g_m2g_{s}", [128, 16], F32))
                varg = EL(nc.sbuf_tensor(f"g_varg_{s}", [128, 16], F32))
                sqg = EL(nc.sbuf_tensor(f"g_sqg_{s}", [128, 16], F32))
                rgrid = EL(nc.sbuf_tensor(f"g_rg_{s}", [128, 16], F32))
                murow = srow
                rrow = qrow
                tmp = EL(nc.sbuf_tensor(f"g_tmp_{s}", [128, 512], F32))
                tmpf = EL(nc.sbuf_tensor(f"g_tmpf_{s}", [128, 512], F32))
                x2 = [EL(nc.sbuf_tensor(f"g_x2{i}_{s}", [128, 512], BF16)) for i in range(2)]
                consts = (ones_col_b, ones_row_f, cpool, srow, qrow, sgrid, qgrid, m2g, varg,
                          sqg, rgrid, murow, rrow, tmp, x2, ps_s, ps_q, ps_c, ps_d)
                for dc in range(ND):
                    h = _inc(dw, 16)
                    ctx.g(lambda e, s=s, dc=dc, h=h:
                          e.dma_start(xn_sb[dc][:, :], xn1_d[s, dc]).then_inc(h, 16))
                ctx.wait("g", dw)
                ctx.wait("t", dw)
                ctx.wait("s", dw)
                emit_ln_stats(ctx, xn_sb, aux_sems, consts)
                nrm_d = aux_sems["nrm"]; bc_d = aux_sems["bc"]; dreg = aux_sems["dreg"]
                ctx.wait("t", dreg)
                for tb in range(4):
                    sl = slice(tb * 512, (tb + 1) * 512)
                    if tb >= 1:
                        ctx.wait("t", nrm_d)
                    h = _inc(bc_d, 1)
                    def bc(e, sl=sl, h=h):
                        e.matmul(ps_c[:, :], ones_row_f[:, :], murow[0:1, sl], start=True, stop=True)
                        e.matmul(ps_d[:, :], ones_row_f[:, :], rrow[0:1, sl], start=True, stop=True).then_inc(h, 1)
                    ctx.t(bc)
                    ctx.wait("v", bc_d)
                    for dc in range(ND):
                        cw0 = nrm_d.v
                        cw1 = _inc(nrm_d, 1) and nrm_d.v
                        cw2 = _inc(nrm_d, 1) and nrm_d.v
                        _inc(nrm_d, 1)
                        h = nrm_d.h
                        def nrm(e, dc=dc, sl=sl, h=h, cw0=cw0, cw1=cw1, cw2=cw2):
                            if cw0 > 0:
                                e.wait_ge(h, cw0)
                            e.tensor_sub(tmp[:, :], xn_sb[dc][:, sl], ps_c[:, :]).then_inc(h, 1)
                            e.wait_ge(h, cw1)
                            e.tensor_mul(tmpf[:, :], tmp[:, :], ps_d[:, :]).then_inc(h, 1)
                            e.wait_ge(h, cw2)
                            e.tensor_scalar(xni[dc][:, sl], tmpf[:, :], lns[:, dc:dc + 1],
                                            lnb[:, dc:dc + 1], ALU.mult, ALU.add).then_inc(h, 1)
                        ctx.v(nrm)
                ctx.wait("t", nrm_d)
                ctx.wait("g", nrm_d)
            with ExitStack() as phV:
                EV = phV.enter_context
                wh = [EV(nc.sbuf_tensor(f"g_wh{c}_{s}", [128, 2048], BF16)) for c in range(ND)]
                wqk = [EV(nc.sbuf_tensor(f"g_wqk{c}_{s}", [128, 128], BF16)) for c in range(ND)]
                qk_sb = EV(nc.sbuf_tensor(f"g_qk_{s}", [128, T], BF16))
                for dc in range(ND):
                    h = _inc(dw, 16)
                    ctx.g(lambda e, dc=dc, h=h:
                          e.dma_start(wh[dc][:, :], whid_d[dc * 128:(dc + 1) * 128, 0:2048]).then_inc(h, 16))
                    h = _inc(dw, 16)
                    ctx.g(lambda e, dc=dc, h=h:
                          e.dma_start(wqk[dc][:, :], wqk_d[dc * 128:(dc + 1) * 128, :]).then_inc(h, 16))
                ctx.wait("g", dw)
                ctx.wait("t", dw)
                for tt in range(NT):
                    g4 = 4 * (tt % 2)
                    if evd.v >= 8:
                        ctx.wait("t", evd, evd.v - 4)
                    for dc in range(ND):
                        st_ = (dc == 0)
                        def vmm(e, dc=dc, tt=tt, g4=g4, st_=st_):
                            for vb in range(4):
                                ins = e.matmul(ps8[g4 + vb][:, :], xni[dc][:, tt * 128:(tt + 1) * 128],
                                         wh[dc][:, vb * 512:(vb + 1) * 512],
                                         start=st_, stop=False)
                                if vb > 0:
                                    _CUR_SKIP_LDW.add(ins.ins.name)
                        ctx.t(vmm)
                    h = _inc(mmd, 1)
                    def vbias(e, tt=tt, g4=g4, h=h):
                        for vb in range(4):
                            ins = e.matmul(ps8[g4 + vb][:, :], ones_row_b[:, :],
                                           vrow_b[0:1, vb * 512:(vb + 1) * 512],
                                           start=False, stop=True)
                            if vb > 0:
                                _CUR_SKIP_LDW.add(ins.ins.name)
                        ins.then_inc(h, 1)
                    ctx.t(vbias)
                    ctx.wait("s", mmd)
                    h = _inc(evd, 4)
                    def vev(e, tt=tt, g4=g4, h=h):
                        for vb in range(4):
                            ins = e.activation(v_sb[tt][:, vb * 512:(vb + 1) * 512], ps8[g4 + vb][:, :],
                                               AF.Silu)
                        ins.then_inc(h, 4)
                    ctx.s(vev)
                for tb in range(4):
                    ps = ps_a if tb % 2 == 0 else ps_b
                    ctx.wait("t", evd, evd.v - 1)
                    for dc in range(ND):
                        st_ = (dc == 0); sp = (dc == ND - 1)
                        if sp:
                            h = _inc(mmd, 1)
                            ctx.t(lambda e, dc=dc, tb=tb, ps=ps, st_=st_, sp=sp, h=h:
                                  e.matmul(ps[:, :], wqk[dc][:, :],
                                           xni[dc][:, tb * 512:(tb + 1) * 512],
                                           start=st_, stop=sp).then_inc(h, 1))
                        else:
                            ctx.t(lambda e, dc=dc, tb=tb, ps=ps, st_=st_, sp=sp:
                                  e.matmul(ps[:, :], wqk[dc][:, :],
                                           xni[dc][:, tb * 512:(tb + 1) * 512],
                                           start=st_, stop=sp))
                    ctx.wait("s", mmd)
                    h = _inc(evd, 1)
                    ctx.s(lambda e, tb=tb, ps=ps, h=h:
                          e.activation(qk_sb[:, tb * 512:(tb + 1) * 512], ps[:, :],
                                       AF.Silu, bias=bqk[:, 0:1]).then_inc(h, 1))
                ctx.wait("v", evd)
                h = _inc(aux, 1)
                def osapp(e, h=h):
                    e.tensor_scalar(qp[:, :], qk_sb[:, :], osc[:, 0:1], osc[:, 1:2], ALU.mult, ALU.add)
                    e.tensor_scalar(kp[:, :], qk_sb[:, :], osc[:, 2:3], osc[:, 3:4], ALU.mult, ALU.add).then_inc(h, 1)
                ctx.v(osapp)
                ctx.wait("t", aux)
                ctx.wait("g", aux)
            def _ib_body(ib):
                isl = slice(ib * 512, (ib + 1) * 512)
                with ExitStack() as phI:
                    EI = phI.enter_context
                    gate = [EI(nc.sbuf_tensor(f"g_gate{fc}_{s}_{ib}", [128, 512], BF16)) for fc in range(16)]
                    with ExitStack() as phG:
                        EG = phG.enter_context
                        whg = [EG(nc.sbuf_tensor(f"g_whg{c}_{s}_{ib}", [128, 2048], BF16)) for c in range(ND)]
                        ctx.wait("g", mmd)
                        for dc in range(ND):
                            h = _inc(dw, 16)
                            ctx.g(lambda e, dc=dc, h=h:
                                  e.dma_start(whg[dc][:, :], whid_d[dc * 128:(dc + 1) * 128, 2048:4096]).then_inc(h, 16))
                        ctx.wait("g", dw)
                        ctx.wait("t", dw)
                        for fc in range(16):
                            ps = ps8[fc % 8]
                            if evd.v >= 8:
                                ctx.wait("t", evd, evd.v - 7)
                            for dc in range(ND):
                                st_ = (dc == 0); sp = (dc == ND - 1)
                                if sp:
                                    h = _inc(mmd, 1)
                                    ctx.t(lambda e, dc=dc, fc=fc, isl=isl, ps=ps, st_=st_, sp=sp, h=h:
                                          e.matmul(ps[:, :], whg[dc][:, fc * 128:(fc + 1) * 128],
                                                   xni[dc][:, isl], start=st_, stop=sp).then_inc(h, 1))
                                else:
                                    ctx.t(lambda e, dc=dc, fc=fc, isl=isl, ps=ps, st_=st_, sp=sp:
                                          e.matmul(ps[:, :], whg[dc][:, fc * 128:(fc + 1) * 128],
                                                   xni[dc][:, isl], start=st_, stop=sp))
                            ctx.wait("s", mmd)
                            h = _inc(evd, 1)
                            ctx.s(lambda e, fc=fc, ps=ps, h=h:
                                  e.activation(gate[fc][:, :], ps[:, :], AF.Silu,
                                               bias=bhg[:, fc:fc + 1]).then_inc(h, 1))
                        ctx.wait("g", evd)
                        ctx.wait("t", evd)
                    og = [EI(nc.sbuf_tensor(f"g_og{fc}_{s}_{ib}", [128, 512], BF16)) for fc in range(16)]
                    phA = ExitStack()
                    EA = phA.enter_context
                    attnT = [EA(nc.sbuf_tensor(f"g_at{jt}_{s}_{ib}", [128, 512], BF16)) for jt in range(NT)]
                    rbuf = [EA(nc.sbuf_tensor(f"g_rb{i}_{s}_{ib}", [128, 512], F32)) for i in range(4)]
                    for jt in range(NT):
                        ps = ps8[jt % 8]
                        if relu_d.v >= 8:
                            ctx.wait("t", relu_d, relu_d.v - 7)
                        h = _inc(mmd, 1)
                        ctx.t(lambda e, jt=jt, isl=isl, ps=ps, h=h:
                              e.matmul(ps[:, :], kp[:, jt * 128:(jt + 1) * 128], qp[:, isl],
                                       start=True, stop=True).then_inc(h, 1))
                        ctx.wait("v", mmd)
                        if sq2_d.v >= 4:
                            ctx.wait("v", sq2_d, sq2_d.v - 3)
                        h = _inc(relu_d, 1)
                        ctx.v(lambda e, jt=jt, ps=ps, h=h:
                              e.tensor_scalar_max(rbuf[jt % 4][:, :], ps[:, :], 0.0).then_inc(h, 1))
                        if sq2_d.v >= 1:
                            ctx.wait("s", relu_d, relu_d.v)
                        else:
                            ctx.wait("s", relu_d)
                        h = _inc(sq2_d, 1)
                        ctx.s(lambda e, jt=jt, h=h:
                              e.activation(attnT[jt][:, :], rbuf[jt % 4][:, :], AF.Square).then_inc(h, 1))
                    ctx.wait("t", sq2_d)
                    for fc in range(16):
                        ps = ps8[fc % 8]
                        if ogd.v >= 8:
                            ctx.wait("t", ogd, ogd.v - 7)
                        for jt in range(NT):
                            st_ = (jt == 0); sp = (jt == NT - 1)
                            if sp:
                                h = _inc(mmd, 1)
                                ctx.t(lambda e, jt=jt, fc=fc, ps=ps, st_=st_, sp=sp, h=h:
                                      e.matmul(ps[:, :], v_sb[jt][:, fc * 128:(fc + 1) * 128],
                                               attnT[jt][:, :], start=st_, stop=sp).then_inc(h, 1))
                            else:
                                ctx.t(lambda e, jt=jt, fc=fc, ps=ps, st_=st_, sp=sp:
                                      e.matmul(ps[:, :], v_sb[jt][:, fc * 128:(fc + 1) * 128],
                                               attnT[jt][:, :], start=st_, stop=sp))
                        ctx.wait("v", mmd)
                        h = _inc(ogd, 1)
                        ctx.v(lambda e, fc=fc, ps=ps, h=h:
                              e.tensor_mul(og[fc][:, :], ps[:, :], gate[fc][:, :]).then_inc(h, 1))
                    ctx.wait("g", ogd)
                    phA.close()
                    with ExitStack() as phW:
                        EW = phW.enter_context
                        wo = [EW(nc.sbuf_tensor(f"g_wo{fc}_{s}_{ib}", [128, 1024], BF16)) for fc in range(16)]
                        ctx.wait("g", mmd)
                        for fc in range(16):
                            h = _inc(dw, 16)
                            ctx.g(lambda e, fc=fc, h=h:
                                  e.dma_start(wo[fc][:, :], wout_d[fc * 128:(fc + 1) * 128, :]).then_inc(h, 16))
                        ctx.wait("g", dw)
                        ctx.wait("t", dw)
                        ctx.wait("t", ogd)
                        for oc in range(ND):
                            ps = ps8[oc % 8]
                            if evd.v >= 8:
                                ctx.wait("t", evd, evd.v - 7)
                            for fc in range(16):
                                st_ = (fc == 0); sp = (fc == 15)
                                if sp:
                                    h = _inc(mmd, 1)
                                    ctx.t(lambda e, fc=fc, oc=oc, ps=ps, st_=st_, sp=sp, h=h:
                                          e.matmul(ps[:, :], wo[fc][:, oc * 128:(oc + 1) * 128],
                                                   og[fc][:, :], start=st_, stop=sp).then_inc(h, 1))
                                else:
                                    ctx.t(lambda e, fc=fc, oc=oc, ps=ps, st_=st_, sp=sp:
                                          e.matmul(ps[:, :], wo[fc][:, oc * 128:(oc + 1) * 128],
                                                   og[fc][:, :], start=st_, stop=sp))
                            ctx.wait("s", mmd)
                            h = _inc(evd, 1)
                            ctx.s(lambda e, oc=oc, isl=isl, ps=ps, h=h:
                                  e.activation(s1_sb[oc][:, isl], ps[:, :], AF.Identity,
                                               bias=bout[:, oc:oc + 1]).then_inc(h, 1))
                        ctx.wait("g", evd)
                        ctx.wait("v", evd)
            for ib in range(4):
                _ib_body(ib)
            ctx.wait("g", evd)
            for dc in range(ND):
                h = _inc(dout, 16)
                ctx.g(lambda e, s=s, dc=dc, h=h:
                      e.dma_start(s1_d[s, dc], s1_sb[dc][:, :]).then_inc(h, 16))
        ctx.barrier()
    for s in range(NSEQ):
        _seq_body4(s)
    ctx.wait("y", dout)
    outer.close()


STAGES["l1attn"] = dict(
    fn=stage_l1attn,
    reads=["xn1", "W_hidden", "W_qk", "W_out", "ln_scale_fm", "ln_bias_fm",
           "b_qk_col", "bh_v_row", "bh_g_fm", "b_out_fm",
           "os_w0", "os_b0", "os_w1", "os_b1"],
    writes=["s1"],
)



def stage_l1ffn(ctx, io):
    nc = ctx.nc
    xn1_d = io["xn1"]; pooled_d = io["pooled"]; x1_d = io["x1_fm"]; s1_d = io["s1"]
    out_d = io["out"]
    ident_d = io["ident_f32"]; e16_d = io["e16"]

    outer = ExitStack()
    E = outer.enter_context
    ones_col_b = E(nc.sbuf_tensor("m_ones_col_b", [128, 1], BF16))
    ones_row_f = E(nc.sbuf_tensor("m_ones_row_f", [1, 128], F32))
    ones_row_b = E(nc.sbuf_tensor("m_ones_row_b", [1, 128], BF16))
    cpool = E(nc.sbuf_tensor("m_cpool", [128, 4], F32))
    idf = E(nc.sbuf_tensor("m_idf", [128, 128], F32))
    e16s = E(nc.sbuf_tensor("m_e16", [8, 128], F32))
    gs = {p: E(nc.sbuf_tensor(f"m_gs_{p}", [128, 1], F32)) for p in "qkvo"}
    wecol_dbg = E(nc.sbuf_tensor("m_wecol_dbg", [128, 4], F32))
    Scol = {p: E(nc.sbuf_tensor(f"m_S_{p}", [128, 1], F32)) for p in "qkvo"}
    bfm = {p: E(nc.sbuf_tensor(f"m_b_{p}", [128, 8], F32)) for p in "qko"}
    vbrow = E(nc.sbuf_tensor("m_vbrow", [1, 1024], BF16))
    Vt = {p: E(nc.sbuf_tensor(f"m_Vt_{p}", [128, 8 * 128], BF16)) for p in "qkvo"}
    Uero = {p: E(nc.sbuf_tensor(f"m_U_{p}", [128, 1024], BF16)) for p in "qkvo"}

    psg = [E(nc.psum_tensor(f"m_psg{i}", [128, 1024], F32)) for i in range(4)]
    ps_S = [psg[0], psg[1]]
    ps_ST = [psg[2][:, 0:512], psg[2][:, 512:1024]]
    ps_ctx = psg[3][0:65, 0:512]
    ps_x = psg[3][:, 512:1024]
    ps4 = [psg[0][:, 0:512], psg[0][:, 512:1024], psg[1][:, 0:512], psg[1][:, 512:1024]]

    dw = ctx.sem("m_dw")
    tld = [ctx.sem("m_tld0"), ctx.sem("m_tld1")]
    sto = [ctx.sem("m_sto0"), ctx.sem("m_sto1")]
    mmd = ctx.sem("m_mmd")
    evd = ctx.sem("m_evd")
    aux = ctx.sem("m_aux")
    smm = ctx.sem("m_smm")
    maxd = ctx.sem("m_maxd")
    strd = ctx.sem("m_strd")
    dreg = ctx.sem("m_dreg")
    stmm = ctx.sem("m_stmm")
    expd = ctx.sem("m_expd")
    ctxd = ctx.sem("m_ctxd")
    evc = ctx.sem("m_evc")
    dreg2 = ctx.sem("m_dreg2")
    dreg3 = ctx.sem("m_dreg3")
    bcd = ctx.sem("m_bcd")
    evb = ctx.sem("m_evb")
    trd = ctx.sem("m_trd")
    dout = ctx.sem("m_dout")

    ctx.g(lambda e: e.memset(ones_col_b[:, :], 1.0))
    ctx.g(lambda e: e.memset(ones_row_f[:, :], 1.0))
    ctx.g(lambda e: e.memset(ones_row_b[:, :], 1.0))
    ctx.g(lambda e: e.memset(cpool[:, 0:1], 0.125))
    ctx.g(lambda e: e.memset(cpool[:, 1:2], -1e30))
    ctx.g(lambda e: e.memset(cpool[:, 2:3], 1.0))
    for dst, key in ((idf, "ident_f32"), (e16s, "e16")):
        h = _inc(dw, 16)
        ctx.g(lambda e, dst=dst, key=key, h=h: e.dma_start(dst[:, :], io[key]).then_inc(h, 16))
    for p in "qkvo":
        h = _inc(dw, 16)
        ctx.g(lambda e, p=p, h=h: e.dma_start(Scol[p][:, :], io[p + "S_col"]).then_inc(h, 16))
        h = _inc(dw, 16)
        ctx.g(lambda e, p=p, h=h: e.dma_start(Vt[p][:, :].rearrange("p (a b) -> p a b", b=128), io[p + "Vt"].rearrange("(a p) b -> p a b", p=128)).then_inc(h, 16))
        h = _inc(dw, 16)
        ctx.g(lambda e, p=p, h=h: e.dma_start(Uero[p][:, :], io[p + "U_ero"]).then_inc(h, 16))
    for p in "qko":
        h = _inc(dw, 16)
        ctx.g(lambda e, p=p, h=h: e.dma_start(bfm[p][:, :], io[p + "b_fm"]).then_inc(h, 16))
    h = _inc(dw, 16)
    ctx.g(lambda e, h=h: e.dma_start(vbrow[:, :], io["vb_row"]).then_inc(h, 16))
    ctx.wait("g", dw)
    ctx.wait("v", dw)
    h = _inc(aux, 1)
    ctx.v(lambda e, h=h: e.tensor_scalar(bfm["q"][:, :], bfm["q"][:, :], cpool[:, 0:1],
                                         None, ALU.mult).then_inc(h, 1))
    ctx.barrier()

    def _seq_body5(s):
        with ExitStack() as seqst:
            ES = seqst.enter_context
            with ExitStack() as phG:
                EG = phG.enter_context
                gw1 = [EG(nc.sbuf_tensor(f"m_gw1{c}_{s}", [128, 256], F32)) for c in range(ND)]
                gw2 = [EG(nc.sbuf_tensor(f"m_gw2{c}_{s}", [128, 8], F32)) for c in range(2)]
                gb1 = EG(nc.sbuf_tensor(f"m_gb1_{s}", [128, 2], F32))
                gb2r = EG(nc.sbuf_tensor(f"m_gb2_{s}", [1, 8], F32))
                pool_sb = EG(nc.sbuf_tensor(f"m_pool_{s}", [128, 8], F32))
                hrelu = EG(nc.sbuf_tensor(f"m_hrelu_{s}", [128, 2], F32))
                lrow = EG(nc.sbuf_tensor(f"m_lrow_{s}", [1, 8], F32))
                srow = EG(nc.sbuf_tensor(f"m_srow_{s}", [1, 16], F32))
                wrow = EG(nc.sbuf_tensor(f"m_wrow_{s}", [1, 8], F32))
                wcol = EG(nc.sbuf_tensor(f"m_wcol_{s}", [8, 1], F32))
                wecol = EG(nc.sbuf_tensor(f"m_wecol_{s}", [128, 1], F32))
                for dc in range(ND):
                    h = _inc(dw, 16)
                    ctx.g(lambda e, s=s, dc=dc, h=h:
                          e.dma_start(pool_sb[:, dc:dc + 1], pooled_d[s, dc]).then_inc(h, 16))
                    h = _inc(dw, 16)
                    ctx.g(lambda e, dc=dc, h=h:
                          e.dma_start(gw1[dc][:, :], io["gw1T"][dc * 128:(dc + 1) * 128, :]).then_inc(h, 16))
                ctx.wait("g", dw)
                for hc in range(2):
                    h = _inc(dw, 16)
                    ctx.g(lambda e, hc=hc, h=h:
                          e.dma_start(gw2[hc][:, :], io["gw2T"][hc * 128:(hc + 1) * 128, :]).then_inc(h, 16))
                h = _inc(dw, 16)
                ctx.g(lambda e, h=h: e.dma_start(gb1[:, :], io["gb1_fm"]).then_inc(h, 16))
                h = _inc(dw, 16)
                ctx.g(lambda e, h=h: e.dma_start(gb2r[:, :], io["gb2_row"]).then_inc(h, 16))
                ctx.wait("g", dw)
                ctx.wait("t", dw)
                for hc in range(2):
                    h = _inc(mmd, 1)
                    def gmm(e, hc=hc, h=h):
                        for dc in range(ND):
                            ins = e.matmul(ps_x[:, hc:hc + 1], gw1[dc][:, hc * 128:(hc + 1) * 128],
                                           pool_sb[:, dc:dc + 1], start=(dc == 0), stop=(dc == ND - 1))
                        ins.then_inc(h, 1)
                    ctx.t(gmm)
                ctx.wait("s", mmd)
                h = _inc(evd, 1)
                ctx.s(lambda e, h=h: e.activation(hrelu[:, :], ps_x[:, 0:2], AF.Relu,
                                                  bias=gb1[:, 0:2].rearrange("p a -> p a")).then_inc(h, 1))
                evd.v -= 1
                ctx.q["s"].pop()
                for hc in range(2):
                    h = _inc(evd, 1)
                    ctx.s(lambda e, hc=hc, h=h:
                          e.activation(hrelu[:, hc:hc + 1], ps_x[:, hc:hc + 1], AF.Relu,
                                       bias=gb1[:, hc:hc + 1]).then_inc(h, 1))
                ctx.wait("t", evd)
                h = _inc(mmd, 1)
                def lmm(e, h=h):
                    e.matmul(ps_x[0:1, 8:16], hrelu[:, 0:1], gw2[0][:, :], start=True, stop=False)
                    e.matmul(ps_x[0:1, 8:16], hrelu[:, 1:2], gw2[1][:, :], start=False, stop=True).then_inc(h, 1)
                ctx.t(lmm)
                ctx.wait("v", mmd)
                def vstep(fn):
                    h = _inc(aux, 1)
                    ctx.v(lambda e, fn=fn, h=h: fn(e).then_inc(h, 1))
                    ctx.wait("v", aux)
                vstep(lambda e: e.tensor_add(lrow[:, :], ps_x[0:1, 8:16], gb2r[:, :]))
                vstep(lambda e: e.tensor_reduce(srow[0:1, 0:1], lrow[:, :], AX.X, ALU.max))
                vstep(lambda e: e.tensor_single_scalar(srow[0:1, 2:10], lrow[:, :], srow[0:1, 0:1], ALU.is_ge))
                vstep(lambda e: e.scalar_tensor_tensor(srow[0:1, 2:10], srow[0:1, 2:10], cpool[0:1, 1:2],
                                                       lrow[:, :], ALU.mult, ALU.add))
                vstep(lambda e: e.tensor_reduce(srow[0:1, 1:2], srow[0:1, 2:10], AX.X, ALU.max))
                vstep(lambda e: e.tensor_sub(srow[0:1, 10:11], srow[0:1, 0:1], srow[0:1, 1:2]))
                ctx.wait("s", aux)
                h = _inc(aux, 1)
                ctx.s(lambda e, h=h: e.activation(srow[0:1, 11:12], srow[0:1, 10:11],
                                                  AF.Sigmoid).then_inc(h, 1))
                ctx.wait("v", aux)
                vstep(lambda e: e.tensor_sub(srow[0:1, 12:13], cpool[0:1, 2:3], srow[0:1, 11:12]))
                vstep(lambda e: e.tensor_single_scalar(wrow[:, :], lrow[:, :], srow[0:1, 0:1], ALU.is_ge))
                vstep(lambda e: e.tensor_single_scalar(srow[0:1, 2:10], srow[0:1, 2:10], srow[0:1, 1:2], ALU.is_ge))
                vstep(lambda e: e.tensor_single_scalar(wrow[:, :], wrow[:, :], srow[0:1, 11:12], ALU.mult))
                vstep(lambda e: e.scalar_tensor_tensor(wrow[:, :], srow[0:1, 2:10], srow[0:1, 12:13],
                                                       wrow[:, :], ALU.mult, ALU.add))
                ctx.wait("y", aux)
                h = _inc(dreg, 16)
                ctx.y(lambda e, h=h: e.dma_start(wcol[:, :], wrow[:, :]).then_inc(h, 16))
                ctx.wait("t", dreg)
                h = _inc(mmd, 1)
                ctx.t(lambda e, h=h: e.matmul(ps_x[:, 2:3], e16s[:, :], wcol[:, :],
                                              start=True, stop=True).then_inc(h, 1))
                ctx.wait("v", mmd)
                vstep(lambda e: e.tensor_copy(wecol[:, :], ps_x[:, 2:3]))
                for p in "qkvo":
                    vstep(lambda e, p=p: e.tensor_mul(gs[p][:, :], wecol[:, :], Scol[p][:, :]))
                ctx.wait("g", aux)
                ctx.wait("t", aux)
            ctx_asm = [ES(nc.sbuf_tensor(f"m_ctx{c}_{s}", [128, T], BF16)) for c in range(ND)]
            pa_stack = ExitStack()
            EPA = pa_stack.enter_context
            xn_sb = [EPA(nc.sbuf_tensor(f"m_xn{c}_{s}", [128, T], BF16)) for c in range(ND)]
            q_sb = [EPA(nc.sbuf_tensor(f"m_q{c}_{s}", [128, T], BF16)) for c in range(ND)]
            k_sb = [EPA(nc.sbuf_tensor(f"m_k{c}_{s}", [128, T], BF16)) for c in range(ND)]
            v_sb = [EPA(nc.sbuf_tensor(f"m_v{t}_{s}", [128, 16 * 65], BF16)) for t in range(NT)]
            for tt in range(NT):
                ctx.g(lambda e, tt=tt: e.memset(
                    v_sb[tt][:, :].rearrange("p (h c) -> p h c", c=65)[:, :, 64:65], 1.0))
            for dc in range(ND):
                h = _inc(dw, 16)
                ctx.g(lambda e, s=s, dc=dc, h=h:
                      e.dma_start(xn_sb[dc][:, :], xn1_d[s, dc]).then_inc(h, 16))
            ctx.wait("g", dw)
            with ExitStack() as phP:
                EP = phP.enter_context
                wm = [EP(nc.sbuf_tensor(f"m_wm{c}_{s}", [128, 1024], BF16)) for c in range(ND)]
                xvs = EP(nc.sbuf_tensor(f"m_xvs_{s}", [128, T], BF16))
                for ip, p in enumerate("qkv"):
                    ctx.wait("g", mmd)
                    for dc in range(ND):
                        h = _inc(dw, 16)
                        ctx.g(lambda e, p=p, dc=dc, h=h:
                              e.dma_start(wm[dc][:, :], io[p + "WmT"][dc * 128:(dc + 1) * 128, :]).then_inc(h, 16))
                    ctx.wait("g", dw)
                    ctx.wait("t", dw)
                    ctx.wait("t", evd)
                    for dc in range(ND):
                        st_ = (dc == 0); sp = (dc == ND - 1)
                        hh = _inc(mmd, 1) if sp else None
                        def xvm(e, p=p, dc=dc, st_=st_, sp=sp, hh=hh):
                            for tb in range(4):
                                ins = e.matmul(ps4[tb][:, :], Vt[p][:, dc * 128:(dc + 1) * 128],
                                               xn_sb[dc][:, tb * 512:(tb + 1) * 512],
                                               start=st_, stop=sp)
                                if tb > 0:
                                    _CUR_SKIP_LDW.add(ins.ins.name)
                            if hh is not None:
                                ins.then_inc(hh, 1)
                        ctx.t(xvm)
                    ctx.wait("v", mmd)
                    h = _inc(evd, 2)
                    def xvev(e, p=p, h=h):
                        e.tensor_scalar(xvs[:, 0:1024], ps_S[0][:, :], gs[p][:, 0:1], None, ALU.mult)
                        e.tensor_scalar(xvs[:, 1024:2048], ps_S[1][:, :], gs[p][:, 0:1], None, ALU.mult).then_inc(h, 2)
                    ctx.v(xvev)
                    ctx.wait("t", evd)
                    if p in "qk":
                        osb = q_sb if p == "q" else k_sb
                        for oc in range(ND):
                            for th in range(2):
                                grp = ps_S[th]
                                if evd.v >= 2:
                                    ctx.wait("t", evd, evd.v - 1)
                                for dc in range(ND):
                                    st_ = (dc == 0)
                                    def mnm(e, dc=dc, oc=oc, th=th, grp=grp, st_=st_):
                                        for tb in range(2):
                                            ins = e.matmul(grp[:, tb * 512:(tb + 1) * 512],
                                                     wm[dc][:, oc * 128:(oc + 1) * 128],
                                                     xn_sb[dc][:, th * 1024 + tb * 512: th * 1024 + (tb + 1) * 512],
                                                     start=st_, stop=False)
                                            if tb > 0:
                                                _CUR_SKIP_LDW.add(ins.ins.name)
                                    ctx.t(mnm)
                                hh = _inc(mmd, 1)
                                def exm(e, p=p, oc=oc, th=th, grp=grp, hh=hh):
                                    for tb in range(2):
                                        ins = e.matmul(grp[:, tb * 512:(tb + 1) * 512],
                                                       Uero[p][:, oc * 128:(oc + 1) * 128],
                                                       xvs[:, th * 1024 + tb * 512: th * 1024 + (tb + 1) * 512],
                                                       start=False, stop=True)
                                        if tb > 0:
                                            _CUR_SKIP_LDW.add(ins.ins.name)
                                    ins.then_inc(hh, 1)
                                ctx.t(exm)
                                ctx.wait("s", mmd)
                                h = _inc(evd, 1)
                                if p == "q":
                                    ctx.s(lambda e, oc=oc, th=th, grp=grp, h=h, osb=osb:
                                          e.activation(osb[oc][:, th * 1024:(th + 1) * 1024], grp[:, :],
                                                       AF.Identity, bias=bfm["q"][:, oc:oc + 1],
                                                       scale=cpool[:, 0:1]).then_inc(h, 1))
                                else:
                                    ctx.s(lambda e, oc=oc, th=th, grp=grp, h=h, osb=osb:
                                          e.activation(osb[oc][:, th * 1024:(th + 1) * 1024], grp[:, :],
                                                       AF.Identity, bias=bfm["k"][:, oc:oc + 1]).then_inc(h, 1))
                    else:
                        for tt in range(NT):
                            grp = ps_S[tt % 2]
                            if evd.v >= 2:
                                ctx.wait("t", evd, evd.v - 1)
                            for dc in range(ND):
                                st_ = (dc == 0)
                                def vmm(e, dc=dc, tt=tt, grp=grp, st_=st_):
                                    for vb in range(2):
                                        ins = e.matmul(grp[:, vb * 512:(vb + 1) * 512],
                                                 xn_sb[dc][:, tt * 128:(tt + 1) * 128],
                                                 wm[dc][:, vb * 512:(vb + 1) * 512],
                                                 start=st_, stop=False)
                                        if vb > 0:
                                            _CUR_SKIP_LDW.add(ins.ins.name)
                                ctx.t(vmm)
                            def vex(e, tt=tt, grp=grp):
                                for vb in range(2):
                                    ins = e.matmul(grp[:, vb * 512:(vb + 1) * 512],
                                             xvs[:, tt * 128:(tt + 1) * 128],
                                             Uero["v"][:, vb * 512:(vb + 1) * 512],
                                             start=False, stop=False)
                                    if vb > 0:
                                        _CUR_SKIP_LDW.add(ins.ins.name)
                            ctx.t(vex)
                            h = _inc(mmd, 1)
                            def vbias(e, tt=tt, grp=grp, h=h):
                                for vb in range(2):
                                    ins = e.matmul(grp[:, vb * 512:(vb + 1) * 512], ones_row_b[:, :],
                                                   vbrow[0:1, vb * 512:(vb + 1) * 512],
                                                   start=False, stop=True)
                                    if vb > 0:
                                        _CUR_SKIP_LDW.add(ins.ins.name)
                                ins.then_inc(h, 1)
                            ctx.t(vbias)
                            ctx.wait("v", mmd)
                            h = _inc(evd, 1)
                            ctx.v(lambda e, tt=tt, grp=grp, h=h:
                                  e.tensor_copy(
                                      v_sb[tt][:, :].rearrange("p (h c) -> p h c", c=65)[:, :, 0:64],
                                      grp[:, :].rearrange("p (h c) -> p h c", c=64)).then_inc(h, 1))
                ctx.wait("t", evd)
                ctx.wait("g", evd)
            ctx.barrier()
            with ExitStack() as phA:
                EA = phA.enter_context
                smax = EA(nc.sbuf_tensor(f"m_smax_{s}", [128, 32], F32))
                smaxC = EA(nc.sbuf_tensor(f"m_smaxC_{s}", [128, 16], F32))
                stT_b = EA(nc.sbuf_tensor(f"m_stT_{s}", [16, 128], BF16))
                crow_b = EA(nc.sbuf_tensor(f"m_crow_{s}", [1, 2048], BF16))
                Pj = [EA(nc.sbuf_tensor(f"m_Pj{i}_{s}", [128, 512], BF16)) for i in range(2)]
                ctxden = EA(nc.sbuf_tensor(f"m_ctxden_{s}", [65, 2048], F32))
                dgrid = EA(nc.sbuf_tensor(f"m_dgrid_{s}", [128, 16], F32))
                rgr2 = EA(nc.sbuf_tensor(f"m_rgr2_{s}", [128, 16], F32))
                rrow2 = EA(nc.sbuf_tensor(f"m_rrow2_{s}", [1, 2048], BF16))
                exp_hist = []
                for hh in range(16):
                    hc = hh // 2; hr = 64 * (hh % 2)
                    for qt in range(NT):
                        for half in range(2):
                            par = half
                            if maxd.v >= 2:
                                ctx.wait("t", maxd, maxd.v - 1)
                            h = _inc(smm, 1)
                            def smmf(e, hc=hc, hr=hr, qt=qt, half=half, par=par, h=h):
                                e.matmul(ps_S[par][:, 0:512],
                                         q_sb[hc][hr:hr + 64, qt * 128:(qt + 1) * 128],
                                         k_sb[hc][hr:hr + 64, half * 1024:half * 1024 + 512],
                                         start=True, stop=True)
                                ins = e.matmul(ps_S[par][:, 512:1024],
                                         q_sb[hc][hr:hr + 64, qt * 128:(qt + 1) * 128],
                                         k_sb[hc][hr:hr + 64, half * 1024 + 512:(half + 1) * 1024],
                                         start=True, stop=True)
                                _CUR_SKIP_LDW.add(ins.ins.name)
                                ins.then_inc(h, 1)
                            ctx.t(smmf)
                            ctx.wait("v", smm)
                            h = _inc(maxd, 1)
                            ctx.v(lambda e, qt=qt, half=half, par=par, h=h:
                                  e.tensor_reduce(smax[:, 2 * qt + half:2 * qt + half + 1],
                                                  ps_S[par][:, :], AX.X, ALU.max,
                                                  negate=True).then_inc(h, 1))
                    ctx.wait("v", maxd)
                    h = _inc(aux, 1)
                    def comb_max(e, h=h):
                        ins = e.tensor_tensor(
                            smaxC[:, :],
                            smax[:, :].rearrange("p (a b) -> p a b", b=2)[:, :, 0:1].rearrange("p a o -> p (a o)"),
                            smax[:, :].rearrange("p (a b) -> p a b", b=2)[:, :, 1:2].rearrange("p a o -> p (a o)"),
                            ALU.min)
                        ins.then_inc(h, 1)
                    ctx.v(comb_max)
                    ctx.wait("t", aux)
                    ctx.wait("t", expd)
                    ctx.wait("t", evb)
                    h = _inc(strd, 1)
                    ctx.t(lambda e, h=h: e.transpose(ps_x[0:16, 0:128],
                                                     smaxC[:, :], idf[:, :]).then_inc(h, 1))
                    ctx.wait("v", strd)
                    ctx.wait("v", dreg)
                    h = _inc(aux, 1)
                    ctx.v(lambda e, h=h: e.tensor_copy(stT_b[:, :], ps_x[0:16, 0:128]).then_inc(h, 1))
                    ctx.wait("y", aux)
                    ctx.wait("y", stmm)
                    h = _inc(dreg, 16)
                    ctx.y(lambda e, h=h: e.dma_start(crow_b[:, :], stT_b[:, :]).then_inc(h, 16))
                    ctx.wait("t", dreg)
                    for iq in range(4):
                        isl = slice(iq * 512, (iq + 1) * 512)
                        exp_tgt = {}
                        def emit_stm(jc, isl=isl, hc=hc, hr=hr):
                            par = jc % 2
                            h = _inc(stmm, 1)
                            def stm(e, hc=hc, hr=hr, jc=jc, isl=isl, par=par, h=h):
                                e.matmul(ps_ST[par][:, :],
                                         k_sb[hc][hr:hr + 64, jc * 128:(jc + 1) * 128],
                                         q_sb[hc][hr:hr + 64, isl], start=True, stop=False)
                                e.matmul(ps_ST[par][:, :], ones_row_b[:, :],
                                         crow_b[0:1, isl], start=False, stop=True).then_inc(h, 1)
                            ctx.t(stm)
                            ctx.wait("s", stmm)
                            if ctxd.v >= 1:
                                ctx.wait("s", ctxd)
                            h = _inc(expd, 1)
                            ctx.s(lambda e, par=par, h=h:
                                  e.activation(Pj[par][:, :], ps_ST[par][:, :], AF.Exp).then_inc(h, 1))
                            exp_tgt[jc] = expd.v
                        emit_stm(0)
                        ctx.wait("t", evc)
                        for jc in range(NT):
                            par = jc % 2
                            if jc + 1 < NT:
                                emit_stm(jc + 1)
                            ctx.wait("t", expd, exp_tgt[jc])
                            h = _inc(ctxd, 1)
                            ctx.t(lambda e, jc=jc, hh=hh, par=par, h=h, st_=(jc == 0), sp=(jc == NT - 1):
                                  e.matmul(ps_ctx[:, :], v_sb[jc][:, 65 * hh:65 * hh + 65],
                                           Pj[par][:, :], start=st_, stop=sp).then_inc(h, 1))
                        ctx.wait("v", ctxd)
                        ctx.wait("v", dreg2)
                        h = _inc(evc, 1)
                        ctx.v(lambda e, isl=isl, h=h:
                              e.tensor_copy(ctxden[:, isl], ps_ctx[:, :]).then_inc(h, 1))
                    ctx.wait("y", evc)
                    ctx.wait("y", aux)
                    h = _inc(dreg2, 16)
                    ctx.y(lambda e, h=h: e.dma_start(dgrid[:, :], ctxden[64:65, :]).then_inc(h, 16))
                    ctx.wait("v", dreg2)
                    ctx.wait("v", dreg3)
                    h = _inc(aux, 1)
                    ctx.v(lambda e, h=h: e.reciprocal(rgr2[:, :], dgrid[:, :]).then_inc(h, 1))
                    ctx.wait("g", aux)
                    ctx.wait("g", bcd)
                    h = _inc(dreg3, 16)
                    ctx.g(lambda e, h=h: e.dma_start(rrow2[:, :], rgr2[:, :]).then_inc(h, 16))
                    ctx.wait("t", dreg3)
                    for iq in range(4):
                        isl = slice(iq * 512, (iq + 1) * 512)
                        ctx.wait("t", evb)
                        h = _inc(bcd, 1)
                        ctx.t(lambda e, isl=isl, h=h:
                              e.matmul(ps_x[0:64, 0:512], ones_row_b[0:1, 0:64],
                                       rrow2[0:1, isl], start=True, stop=True).then_inc(h, 1))
                        ctx.wait("v", bcd)
                        h = _inc(evb, 1)
                        ctx.v(lambda e, hc=hc, hr=hr, isl=isl, h=h:
                              e.tensor_mul(ctx_asm[hc][hr:hr + 64, isl], ctxden[0:64, isl],
                                           ps_x[0:64, 0:512]).then_inc(h, 1))
            ctx.barrier()
            pa_stack.close()
            with ExitStack() as phO:
                EO = phO.enter_context
                wmo = [EO(nc.sbuf_tensor(f"m_owm{c}_{s}", [128, 1024], BF16)) for c in range(ND)]
                xvso = EO(nc.sbuf_tensor(f"m_oxvs_{s}", [128, T], BF16))
                x2_sb = [EO(nc.sbuf_tensor(f"m_x2{c}_{s}", [128, T], F32)) for c in range(ND)]
                g1b = [EO(nc.sbuf_tensor(f"m_g1{i}_{s}", [128, 512], BF16)) for i in range(2)]
                s1t = [EO(nc.sbuf_tensor(f"m_s1t{i}_{s}", [128, 512], BF16)) for i in range(2)]
                x1t = [EO(nc.sbuf_tensor(f"m_x1t{i}_{s}", [128, 512], F32)) for i in range(2)]
                mrow = EO(nc.sbuf_tensor(f"m_mrow_{s}", [128, 512], F32))
                otok = [EO(nc.sbuf_tensor(f"m_otok{i}_{s}", [128, 1024], F32)) for i in range(2)]
                for dc in range(ND):
                    h = _inc(dw, 16)
                    ctx.g(lambda e, dc=dc, h=h:
                          e.dma_start(wmo[dc][:, :], io["oWmT"][dc * 128:(dc + 1) * 128, :]).then_inc(h, 16))
                ctx.wait("g", dw)
                ctx.wait("t", dw)
                for dc in range(ND):
                    st_ = (dc == 0); sp = (dc == ND - 1)
                    hh = _inc(mmd, 1) if sp else None
                    def oxvm(e, dc=dc, st_=st_, sp=sp, hh=hh):
                        for tb in range(4):
                            ins = e.matmul(ps4[tb][:, :], Vt["o"][:, dc * 128:(dc + 1) * 128],
                                           ctx_asm[dc][:, tb * 512:(tb + 1) * 512],
                                           start=st_, stop=sp)
                            if tb > 0:
                                _CUR_SKIP_LDW.add(ins.ins.name)
                        if hh is not None:
                            ins.then_inc(hh, 1)
                    ctx.t(oxvm)
                ctx.wait("v", mmd)
                h = _inc(evd, 2)
                def oxvev(e, h=h):
                    e.tensor_scalar(xvso[:, 0:1024], ps_S[0][:, :], gs["o"][:, 0:1], None, ALU.mult)
                    e.tensor_scalar(xvso[:, 1024:2048], ps_S[1][:, :], gs["o"][:, 0:1], None, ALU.mult).then_inc(h, 2)
                ctx.v(oxvev)
                ctx.wait("t", evd)
                for oc in range(ND):
                    for tb in range(4):
                        par = (oc * 4 + tb) % 2
                        ps = ps_S[par]
                        if evd.v >= 2:
                            ctx.wait("t", evd, evd.v - 1)
                        for dc in range(ND):
                            ctx.t(lambda e, dc=dc, oc=oc, tb=tb, ps=ps:
                                  e.matmul(ps[:, 0:512], wmo[dc][:, oc * 128:(oc + 1) * 128],
                                           ctx_asm[dc][:, tb * 512:(tb + 1) * 512],
                                           start=(dc == 0), stop=False))
                        h = _inc(mmd, 1)
                        ctx.t(lambda e, oc=oc, tb=tb, ps=ps, h=h:
                              e.matmul(ps[:, 0:512], Uero["o"][:, oc * 128:(oc + 1) * 128],
                                       xvso[:, tb * 512:(tb + 1) * 512],
                                       start=False, stop=True).then_inc(h, 1))
                        if aux.v >= 2:
                            ctx.wait("g", aux, aux.v - 1)
                        h = _inc(tld[par], 16)
                        ctx.g(lambda e, s=s, oc=oc, tb=tb, par=par, h=h:
                              e.dma_start(s1t[par][:, :], s1_d[s, oc, :, tb * 512:(tb + 1) * 512]).then_inc(h, 16))
                        h = _inc(tld[par], 16)
                        ctx.g(lambda e, s=s, oc=oc, tb=tb, par=par, h=h:
                              e.dma_start(x1t[par][:, :], x1_d[s, oc, :, tb * 512:(tb + 1) * 512]).then_inc(h, 16))
                        if aux.v >= 2:
                            ctx.wait("s", aux, aux.v - 1)
                        ctx.wait("s", mmd)
                        h = _inc(evd, 1)
                        ctx.s(lambda e, oc=oc, par=par, ps=ps, h=h:
                              e.activation(g1b[par][:, :], ps[:, 0:512], AF.Silu,
                                           bias=bfm["o"][:, oc:oc + 1]).then_inc(h, 1))
                        ctx.wait("v", evd)
                        ctx.wait("v", tld[par])
                        cw0 = aux.v
                        cw1 = _inc(aux, 1) and aux.v
                        _inc(aux, 1)
                        h = aux.h
                        def comb(e, oc=oc, tb=tb, par=par, h=h, cw0=cw0, cw1=cw1):
                            if cw0 > 0:
                                e.wait_ge(h, cw0)
                            e.tensor_mul(mrow[:, :], g1b[par][:, :], s1t[par][:, :]).then_inc(h, 1)
                            e.wait_ge(h, cw1)
                            e.tensor_add(x2_sb[oc][:, tb * 512:(tb + 1) * 512], x1t[par][:, :],
                                         mrow[:, :]).then_inc(h, 1)
                        ctx.v(comb)
                ctx.wait("t", aux)
                for tt in range(NT):
                    par = tt % 2
                    for half in range(2):
                        hpar = half
                        if trd.v >= 2:
                            ctx.wait("t", trd, trd.v - 1)
                        h = _inc(strd, 1)
                        def trx(e, tt=tt, half=half, h=h):
                            for j in range(4):
                                dc = half * 4 + j
                                ins = e.transpose(ps_S[half][:, j * 128:(j + 1) * 128],
                                                  x2_sb[dc][:, tt * 128:(tt + 1) * 128], idf[:, :])
                            ins.then_inc(h, 1)
                        ctx.t(trx)
                        ctx.wait("v", strd)
                        ctx.wait("v", sto[par])
                        h = _inc(trd, 1)
                        def evt(e, tt=tt, half=half, par=par, h=h):
                            for j in range(4):
                                dc = half * 4 + j
                                ins = e.tensor_copy(otok[par][:, dc * 128:(dc + 1) * 128],
                                                    ps_S[half][:, j * 128:(j + 1) * 128])
                            ins.then_inc(h, 1)
                        ctx.v(evt)
                    ctx.wait("y", trd)
                    h = _inc(sto[par], 16)
                    ctx.y(lambda e, s=s, tt=tt, par=par, h=h:
                          e.dma_start(out_d[s, tt * 128:(tt + 1) * 128, :], otok[par][:, :]).then_inc(h, 16))
        ctx.barrier()
    for s in range(NSEQ):
        _seq_body5(s)
    ctx.wait("y", dout)
    ctx.wait("y", sto[0])
    ctx.wait("y", sto[1])
    outer.close()


STAGES["l1ffn"] = dict(
    fn=stage_l1ffn,
    reads=["xn1", "pooled", "x1_fm", "s1", "ident_f32", "e16",
           "gw1T", "gb1_fm", "gw2T", "gb2_row",
           "qWmT", "qVt", "qU_ero", "qS_col", "qb_fm",
           "kWmT", "kVt", "kU_ero", "kS_col", "kb_fm",
           "vWmT", "vVt", "vU_ero", "vS_col", "vb_row",
           "oWmT", "oVt", "oU_ero", "oS_col", "ob_fm"],
    writes=["out"],
)
